# version 04
# baseline (speedup 1.0000x reference)
import numpy as np
import ml_dtypes
from contextlib import ExitStack
import concourse.bass as bass
import concourse.mybir as mybir
from concourse.bass_utils import run_bass_kernel_spmd

F32 = mybir.dt.float32
BF16 = mybir.dt.bfloat16
AF = mybir.ActivationFunctionType
ALU = mybir.AluOpType
AX = mybir.AxisListType

P = 128
TOK = 1024
NT = TOK // P
D = 4096
IC = D // P
S = 8192
BLK = 512
NBLK = S // BLK
NEG = -1.0e30

L0 = dict(H=32, dh=128, hd=4096, oqk=12288, scale=1.0 / 128.0, alibi=True, C=20.0, G=2)
L1 = dict(H=8, dh=256, hd=2048, oqk=6144, scale=1.0, alibi=False, C=90.0, G=1)

ENGS = ("sync", "tensor", "vector", "scalar", "gpsimd")


def alibi_slopes(n):
    import math
    cp2 = 2 ** math.floor(math.log2(n))
    base = 2 ** (-2 ** (-(math.log2(cp2) - 3)))
    sl = base ** np.arange(1, cp2 + 1, dtype=np.float32)
    if cp2 != n:
        eb = 2 ** (-2 ** (-(math.log2(2 * cp2) - 3)))
        n_rem = min(cp2, n - cp2)
        extra = eb ** np.arange(1, 1 + 2 * n_rem, 2, dtype=np.float32)
        sl = np.concatenate([sl, extra])
    return sl.astype(np.float32)


SLOPES = alibi_slopes(32)


class Em:

    def __init__(self):
        self.ops = []
        self.cnt = {}
        self.lastwait = {e: {} for e in ENGS}

    def tok(self, sem):
        return (sem, self.cnt.get(sem, 0))

    AUTO = {"vector": "s_v", "scalar": "s_a"}

    def op(self, eng, fn, wait=(), inc=None, incby=1):
        if eng in self.AUTO:
            auto = self.AUTO[eng]
            wait = list(wait) + [(auto, self.cnt.get(auto, 0))]
            inc, incby = auto, 1
        waits = []
        for wt in wait:
            if wt is None:
                continue
            sem, c = wt
            if c <= 0 or self.lastwait[eng].get(sem, -1) >= c:
                continue
            self.lastwait[eng][sem] = c
            waits.append((sem, c))
        ret = None
        if inc is not None:
            self.cnt[inc] = self.cnt.get(inc, 0) + incby
            ret = (inc, self.cnt[inc])

        def run(engine, sems, fn=fn, waits=waits, inc=inc, incby=incby):
            for s0, c0 in waits:
                engine.wait_ge(sems[s0], c0)
            r = fn(engine, sems)
            if inc is not None:
                r.then_inc(sems[inc], incby)

        self.ops.append((eng, run))
        return ret

    def flush(self, nc, sems):
        if not self.ops:
            return
        bes = ExitStack()
        with bes:
            block = bes.enter_context(nc.Block())
            for name in ENGS:
                ops = [fn for e, fn in self.ops if e == name]
                if not ops:
                    continue
                reg = {"sync": block.sync, "tensor": block.tensor,
                       "vector": block.vector, "scalar": block.scalar,
                       "gpsimd": block.gpsimd}[name]

                def body(engine, ops=ops):
                    for fn in ops:
                        fn(engine, sems)
                reg(body)
        self.ops = []


def make_sems(nc, es, names):
    return {n: es.enter_context(nc.semaphore(n)) for n in names}



def emit_ln(em, src, xn_dram, xn_fm, bufs):
    xt, xc, sqs, xnt, mu, vs, rstd, eps = bufs
    teps = em.op("gpsimd", lambda e, s: e.memset(eps[:, 0:1], 1e-5), inc="s_g")
    last_store = None
    for t in range(NT):
        tl = em.op("sync", lambda e, s, t=t: e.dma_start(
            out=xt[:], in_=src[t * P:(t + 1) * P, :]),
            wait=[last_store], inc="s_ld", incby=16)
        em.op("vector", lambda e, s: e.reduce_sum(mu[:, 0:1], xt[:], AX.X), wait=[tl])
        em.op("vector", lambda e, s: e.tensor_scalar_mul(mu[:, 0:1], mu[:, 0:1], 1.0 / D))
        txc = em.op("vector", lambda e, s: e.tensor_scalar(
            xc[:], xt[:], mu[:, 0:1], None, ALU.subtract), inc="s_v")
        tsq = em.op("scalar", lambda e, s: e.activation(
            sqs[:], xc[:], AF.Square), wait=[txc], inc="s_a")
        tv = em.op("vector", lambda e, s: e.reduce_sum(
            vs[:, 0:1], sqs[:], AX.X), wait=[tsq], inc="s_v")
        em.op("scalar", lambda e, s: e.activation(
            rstd[:, 0:1], vs[:, 0:1], AF.Ln, bias=eps[:, 0:1], scale=1.0 / D),
            wait=[tv, teps])
        ta = em.op("scalar", lambda e, s: e.activation(
            rstd[:, 0:1], rstd[:, 0:1], AF.Exp, bias=0.0, scale=-0.5), inc="s_a")
        tx = em.op("vector", lambda e, s: e.tensor_scalar(
            xnt[:], xc[:], rstd[:, 0:1], None, ALU.mult), wait=[ta], inc="s_v")
        last_store = em.op("sync", lambda e, s, t=t: e.dma_start(
            out=xn_dram[t * P:(t + 1) * P, :], in_=xnt[:]), wait=[tx], inc="s_st", incby=16)
    fm_tok = None
    if xn_fm is not None:
        for ic in range(IC):
            fm_tok = em.op("sync", lambda e, s, ic=ic: e.dma_start_transpose(
                xn_fm[:, ic * TOK:(ic + 1) * TOK], xn_dram[:, ic * P:(ic + 1) * P]),
                wait=[last_store], inc="s_fm", incby=16)
    return last_store, fm_tok


def emit_qkv(em, Lp, wqkv_dram, barr_sb, barr_tok, xn_fm, fm_tok, wbuf, stg, ps,
             q_out, k_out, vfm_dram):
    hd, oqk = Lp["hd"], Lp["oqk"]
    noc, nQ = oqk // P, hd // P
    mm_toks, ev_toks, st_toks = {}, {}, {}
    for oc in range(noc):
        par = oc % 2
        lastw = None
        for jj in range(4):
            lastw = em.op("gpsimd", lambda e, s, oc=oc, par=par, jj=jj: e.dma_start(
                out=wbuf[par][:].rearrange("p (a o) -> p a o", o=P)[:, jj * 8:(jj + 1) * 8, :],
                in_=wqkv_dram.rearrange("(a p) o -> p a o", p=P)[:, jj * 8:(jj + 1) * 8, oc * P:(oc + 1) * P]),
                wait=[mm_toks.get((oc - 2, 1))] if jj == 0 else [], inc="s_w", incby=16)
        for th in range(2):
            step = 2 * oc + th
            pp = th

            def mms(e, s, par=par, pp=pp, th=th):
                r = None
                for ic in range(IC):
                    r = e.matmul(ps[pp][:],
                                 wbuf[par][:, ic * P:(ic + 1) * P],
                                 xn_fm[:, ic * TOK + th * BLK: ic * TOK + th * BLK + BLK],
                                 start=(ic == 0), stop=(ic == IC - 1))
                return r
            tmm = em.op("tensor", mms, wait=[lastw, fm_tok, ev_toks.get(step - 2)], inc="s_mm")
            mm_toks[(oc, th)] = tmm
            tev = em.op("scalar", lambda e, s, pp=pp, oc=oc: e.activation(
                stg[pp][:], ps[pp][:], AF.Identity, bias=barr_sb[:, oc:oc + 1], scale=1.0),
                wait=[tmm, st_toks.get(step - 2), barr_tok], inc="s_ev")
            ev_toks[step] = tev
            if oc < nQ:
                dst, row = q_out, oc * P
            elif oc < 2 * nQ:
                dst, row = k_out, (oc - nQ) * P
            else:
                dst, row = vfm_dram, (oc - 2 * nQ) * P
            st_toks[step] = em.op("sync", lambda e, s, dst=dst, row=row, th=th, pp=pp: e.dma_start(
                out=dst[row:row + P, th * BLK:(th + 1) * BLK], in_=stg[pp][:]),
                wait=[tev], inc="s_st", incby=16)
    return st_toks[2 * noc - 1], ev_toks[2 * noc - 1], ev_toks[2 * noc - 2]


def emit_v_transpose(em, Lp, vfm_dram, v_out, vstg, vdone_tok):
    last = {}
    for tt in range(NT):
        par = tt % 2
        ttr = em.op("sync", lambda e, s, par=par, tt=tt: e.dma_start_transpose(
            vstg[par][:], vfm_dram[:, tt * P:(tt + 1) * P]),
            wait=[vdone_tok, last.get(tt - 2)], inc="s_vt", incby=16)
        last[tt] = em.op("sync", lambda e, s, par=par, tt=tt: e.dma_start(
            out=v_out[tt * P:(tt + 1) * P, :], in_=vstg[par][:]),
            wait=[ttr], inc="s_vt", incby=16)
    return last[NT - 1]


def emit_moe(nc, em, scope, xn_fm, fm_tok, wrT, aT, bP, oneh, g_out, ps, psum_free):
    wr_sb = scope.enter_context(nc.sbuf_tensor("wr_sb", [P, IC * 4], BF16))
    aT_sb = scope.enter_context(nc.sbuf_tensor("aT_sb", [P, IC * 64], BF16))
    bP_sb = scope.enter_context(nc.sbuf_tensor("bP_sb", [64, D], BF16))
    oneh_sb = scope.enter_context(nc.sbuf_tensor("oneh_sb", [4, 64], F32))
    ident = scope.enter_context(nc.sbuf_tensor("identM", [P, P], F32))
    lg = scope.enter_context(nc.sbuf_tensor("lg", [P, 4], F32))
    pmin = scope.enter_context(nc.sbuf_tensor("pmin", [P, 8], F32))
    nm1 = scope.enter_context(nc.sbuf_tensor("nm1", [P, 1], F32))
    t2s = scope.enter_context(nc.sbuf_tensor("t2s", [P, 1], F32))
    ex4 = scope.enter_context(nc.sbuf_tensor("ex4", [P, 4], F32))
    ind4 = scope.enter_context(nc.sbuf_tensor("ind4", [P, 4], F32))
    den = scope.enter_context(nc.sbuf_tensor("den", [P, 1], F32))
    rec = scope.enter_context(nc.sbuf_tensor("rec", [P, 1], F32))
    g4 = scope.enter_context(nc.sbuf_tensor("g4", [P, 4], F32))
    gates_fm = scope.enter_context(nc.sbuf_tensor("gates_fm", [4, TOK], F32))
    gbc_sb = scope.enter_context(nc.sbuf_tensor("gbc_sb", [64, BLK], F32))
    smid = scope.enter_context(nc.sbuf_tensor("smid", [64, BLK], BF16))
    gstg = [scope.enter_context(nc.sbuf_tensor(f"gstg{j}", [P, BLK], BF16)) for j in range(2)]
    ps_l = scope.enter_context(nc.psum_tensor("ps_l", [P, BLK], F32))
    ps_t = scope.enter_context(nc.psum_tensor("ps_t", [P, BLK], F32))
    ps_m = ps[0][0:64, :]
    ps_g = ps[1][0:64, :]

    t1 = em.op("gpsimd", lambda e, s: e.dma_start(
        out=wr_sb[:].rearrange("p (a r) -> p a r", r=4),
        in_=wrT.rearrange("(a p) r -> p a r", p=P)), inc="s_w", incby=16)
    t2_ = em.op("gpsimd", lambda e, s: e.dma_start(
        out=aT_sb[:].rearrange("p (a r) -> p a r", r=64),
        in_=aT.rearrange("(a p) r -> p a r", p=P)), inc="s_w", incby=16)
    t3 = em.op("gpsimd", lambda e, s: e.dma_start(out=bP_sb[:], in_=bP), inc="s_w", incby=16)
    t4 = em.op("sync", lambda e, s: e.dma_start(out=oneh_sb[:], in_=oneh), inc="s_ld", incby=16)
    tms = em.op("gpsimd", lambda e, s: e.memset(ident[:], 0.0), inc="s_g")
    ti = em.op("gpsimd", lambda e, s: e.affine_select(
        out=ident[:], in_=ident[:], compare_op=ALU.not_equal, fill=1.0,
        base=0, pattern=[[-1, P]], channel_multiplier=1), wait=[tms], inc="s_g")

    pairs = [(0, 1), (0, 2), (0, 3), (1, 2), (1, 3), (2, 3)]
    gtok = {}
    prev_tr = None
    for t in range(NT):
        def lmm(e, s, t=t):
            r = None
            for ic in range(IC):
                r = e.matmul(ps_l[:, 0:4],
                             xn_fm[:, ic * TOK + t * P: ic * TOK + (t + 1) * P],
                             wr_sb[:, ic * 4:(ic + 1) * 4],
                             start=(ic == 0), stop=(ic == IC - 1))
            return r
        tml = em.op("tensor", lmm, wait=[t1, fm_tok, gtok.get(t - 1)], inc="s_mm")
        em.op("vector", lambda e, s: e.tensor_copy(lg[:], ps_l[:, 0:4]), wait=[tml])
        for i, (a, b) in enumerate(pairs):
            em.op("vector", lambda e, s, i=i, a=a, b=b: e.tensor_tensor(
                pmin[:, i:i + 1], lg[:, a:a + 1], lg[:, b:b + 1], ALU.min))
        em.op("vector", lambda e, s: e.tensor_tensor(pmin[:, 6:7], pmin[:, 0:1], pmin[:, 1:2], ALU.max))
        for i in (2, 3, 4):
            em.op("vector", lambda e, s, i=i: e.tensor_tensor(
                pmin[:, 6:7], pmin[:, 6:7], pmin[:, i:i + 1], ALU.max))
        em.op("vector", lambda e, s: e.tensor_tensor(t2s[:, 0:1], pmin[:, 6:7], pmin[:, 5:6], ALU.max))
        em.op("vector", lambda e, s: e.tensor_reduce(nm1[:, 0:1], lg[:], AX.X, ALU.max, negate=True))
        tind = em.op("vector", lambda e, s: e.tensor_scalar(
            ind4[:], lg[:], t2s[:, 0:1], None, ALU.is_ge), inc="s_v")
        tex = em.op("scalar", lambda e, s: e.activation(
            ex4[:], lg[:], AF.Exp, bias=nm1[:, 0:1], scale=1.0), wait=[tind], inc="s_a")
        em.op("vector", lambda e, s: e.tensor_tensor(g4[:], ind4[:], ex4[:], ALU.mult), wait=[tex])
        em.op("vector", lambda e, s: e.reduce_sum(den[:, 0:1], g4[:], AX.X))
        em.op("vector", lambda e, s: e.reciprocal(rec[:, 0:1], den[:, 0:1]))
        tg4 = em.op("vector", lambda e, s: e.tensor_scalar(
            g4[:], g4[:], rec[:, 0:1], None, ALU.mult), inc="s_v")
        ttr = em.op("tensor", lambda e, s: e.transpose(ps_t[0:4, 0:P], g4[:], ident[:]),
                    wait=[tg4, ti, prev_tr], inc="s_mm")
        prev_tr = gtok[t] = em.op("vector", lambda e, s, t=t: e.tensor_copy(
            gates_fm[:, t * P:(t + 1) * P], ps_t[0:4, 0:P]), wait=[ttr], inc="s_v")

    dguard = {}
    last = None
    for th in range(2):
        def midmm(e, s, th=th):
            r = None
            for ic in range(IC):
                r = e.matmul(ps_m, aT_sb[:, ic * 64:(ic + 1) * 64],
                             xn_fm[:, ic * TOK + th * BLK: ic * TOK + th * BLK + BLK],
                             start=(ic == 0), stop=(ic == IC - 1))
            return r
        tmid = em.op("tensor", midmm, wait=[t2_, gtok[NT - 1]] + psum_free, inc="s_mm")
        tgb = em.op("tensor", lambda e, s, th=th: e.matmul(
            ps_g, oneh_sb[:], gates_fm[:, th * BLK:(th + 1) * BLK],
            start=True, stop=True), wait=[t4], inc="s_mm")
        em.op("vector", lambda e, s: e.tensor_copy(gbc_sb[:], ps_g),
              wait=[tgb, dguard.get(("sm", th - 1))])
        tsm = em.op("vector", lambda e, s: e.scalar_tensor_tensor(
            out=smid[:], in0=ps_m, scalar=4.0, in1=gbc_sb[:],
            op0=ALU.mult, op1=ALU.mult), wait=[tmid], inc="s_v")
        dguard[("sm", th)] = tsm
        for oc in range(IC):
            pp = oc % 2
            pdst = ps_l if pp == 0 else ps_t
            tdm = em.op("tensor", lambda e, s, oc=oc, pdst=pdst: e.matmul(
                pdst[:], bP_sb[:, oc * P:(oc + 1) * P], smid[:],
                start=True, stop=True),
                wait=[tsm, t3, dguard.get((th, oc - 2))], inc="s_mm")
            tev = em.op("scalar", lambda e, s, pdst=pdst, pp=pp: e.activation(
                gstg[pp][:], pdst[:], AF.Silu),
                wait=[tdm, dguard.get(("st", th, oc - 2))], inc="s_a")
            dguard[(th, oc)] = tev
            last = em.op("sync", lambda e, s, oc=oc, th=th, pp=pp: e.dma_start(
                out=g_out[oc * P:(oc + 1) * P, th * BLK:(th + 1) * BLK],
                in_=gstg[pp][:]), wait=[tev], inc="s_st", incby=16)
            dguard[("st", th, oc)] = last
    return last


def build_launch_A(layer, debug=False):
    Lp = L0 if layer == 0 else L1
    nc = bass.Bass("TRN2", target_bir_lowering=False)
    src = nc.dram_tensor("src", [TOK, D], F32, kind="ExternalInput").ap()
    hd, oqk = Lp["hd"], Lp["oqk"]
    noc = oqk // P
    wqkv = nc.dram_tensor("wqkv", [D, oqk], F32, kind="ExternalInput").ap()
    barr = nc.dram_tensor("barr", [P, noc], F32, kind="ExternalInput").ap()
    if layer == 0:
        wffn = nc.dram_tensor("wffn", [D, D], F32, kind="ExternalInput").ap()
        wffnT = nc.dram_tensor("wffnT", [D, D], F32, kind="ExternalInput").ap()
    else:
        wrT = nc.dram_tensor("wrT", [D, 4], F32, kind="ExternalInput").ap()
        aT = nc.dram_tensor("aT", [D, 64], F32, kind="ExternalInput").ap()
        bP = nc.dram_tensor("bP", [64, D], F32, kind="ExternalInput").ap()
        oneh = nc.dram_tensor("oneh", [4, 64], F32, kind="ExternalInput").ap()

    q_out = nc.dram_tensor("q_out", [hd, TOK], BF16, kind="ExternalOutput").ap()
    k_out = nc.dram_tensor("k_out", [hd, TOK], BF16, kind="ExternalOutput").ap()
    v_out = nc.dram_tensor("v_out", [TOK, hd], BF16, kind="ExternalOutput").ap()
    g_out = nc.dram_tensor("g_out", [D, TOK], BF16, kind="ExternalOutput").ap()
    xn_dram = nc.dram_tensor("xn_dram", [TOK, D], BF16, kind="ExternalOutput" if debug else "Internal").ap()
    vfm_dram = nc.dram_tensor("vfm_dram", [hd, TOK], BF16, kind="Internal").ap()

    es = ExitStack()
    with es:
        wbuf = [es.enter_context(nc.sbuf_tensor(f"wbuf{j}", [P, IC * P], BF16)) for j in range(2)]
        stg = [es.enter_context(nc.sbuf_tensor(f"stg{j}", [P, BLK], BF16)) for j in range(2)]
        vstg = [es.enter_context(nc.sbuf_tensor(f"vstg{j}", [P, hd], BF16)) for j in range(2)]
        barr_sb = es.enter_context(nc.sbuf_tensor("barr_sb", [P, noc], F32))
        ps = [es.enter_context(nc.psum_tensor(f"ps{j}", [P, BLK], F32)) for j in range(2)]
        sems = make_sems(nc, es, ["s_ld", "s_v", "s_a", "s_st", "s_fm", "s_w",
                                  "s_mm", "s_ev", "s_vt", "s_g"])
        em = Em()
        barr_tok = em.op("sync", lambda e, s: e.dma_start(out=barr_sb[:], in_=barr),
                         inc="s_ld", incby=16)

        s1 = ExitStack()
        with s1:
            xn_fm = s1.enter_context(nc.sbuf_tensor("xn_fm", [P, IC * TOK], BF16))
            xt = s1.enter_context(nc.sbuf_tensor("xt", [P, D], F32))
            xc = s1.enter_context(nc.sbuf_tensor("xc", [P, D], F32))
            sqs = s1.enter_context(nc.sbuf_tensor("sqs", [P, D], F32))
            xnt = s1.enter_context(nc.sbuf_tensor("xnt", [P, D], BF16))
            mu = s1.enter_context(nc.sbuf_tensor("mu", [P, 1], F32))
            vs = s1.enter_context(nc.sbuf_tensor("vs", [P, 1], F32))
            rstd = s1.enter_context(nc.sbuf_tensor("rstd", [P, 1], F32))
            epsb = s1.enter_context(nc.sbuf_tensor("epsb", [P, 1], F32))

            _, fm_tok = emit_ln(em, src, xn_dram, xn_fm, (xt, xc, sqs, xnt, mu, vs, rstd, epsb))
            vlast, evl, evl2 = emit_qkv(em, Lp, wqkv, barr_sb, barr_tok, xn_fm, fm_tok,
                                        wbuf, stg, ps, q_out, k_out, vfm_dram)
            emit_v_transpose(em, Lp, vfm_dram, v_out, vstg, vlast)
            if layer == 1:
                emit_moe(nc, em, s1, xn_fm, fm_tok, wrT, aT, bP, oneh, g_out, ps,
                         [evl, evl2])
            em.flush(nc, sems)

        if layer == 0:
            s2 = ExitStack()
            with s2:
                sx_fm = s2.enter_context(nc.sbuf_tensor("sx_fm", [P, IC * TOK], BF16))
                fmchunk = s2.enter_context(nc.sbuf_tensor("fmchunk", [P, TOK], BF16))
                wnat = s2.enter_context(nc.sbuf_tensor("wnat", [P, D], BF16))
                snat = s2.enter_context(nc.sbuf_tensor("snat", [P, D], BF16))
                absout = s2.enter_context(nc.sbuf_tensor("absout", [P, D], BF16))
                swTr = s2.enter_context(nc.sbuf_tensor("swTr", [P, IC * P], BF16))
                swTs = [s2.enter_context(nc.sbuf_tensor(f"swTs{j}", [P, IC * P], BF16)) for j in range(2)]
                scale_arr = s2.enter_context(nc.sbuf_tensor("scale_arr", [P, IC], F32))
                asum = s2.enter_context(nc.sbuf_tensor("asum", [P, 1], F32))

                sg_tok = None
                for ic in range(IC):
                    tt = em.op("sync", lambda e, s, ic=ic: e.dma_start_transpose(
                        fmchunk[:], xn_dram[:, ic * P:(ic + 1) * P]),
                        wait=[sg_tok], inc="s_fm", incby=16)
                    sg_tok = em.op("scalar", lambda e, s, ic=ic: e.activation(
                        sx_fm[:, ic * TOK:(ic + 1) * TOK], fmchunk[:], AF.Sign),
                        wait=[tt], inc="s_a")

                mmk, evk, stk = {}, {}, {}
                tabs_prev, tsg_prev = None, None
                for oc in range(IC):
                    par = oc % 2
                    tn = em.op("gpsimd", lambda e, s, oc=oc: e.dma_start(
                        out=wnat[:], in_=wffn[oc * P:(oc + 1) * P, :]),
                        wait=[tabs_prev], inc="s_w", incby=16)
                    tsn = em.op("scalar", lambda e, s: e.activation(
                        absout[:], wnat[:], AF.Abs), wait=[tn], inc="s_a")
                    tabs_prev = em.op("vector", lambda e, s: e.reduce_sum(
                        asum[:, 0:1], absout[:], AX.X), wait=[tsn], inc="s_v")
                    tsc = em.op("vector", lambda e, s, oc=oc: e.tensor_scalar_mul(
                        scale_arr[:, oc:oc + 1], asum[:, 0:1], 1.0 / D), inc="s_g")
                    lw = None
                    for jj in range(4):
                        lw = em.op("gpsimd", lambda e, s, oc=oc, jj=jj: e.dma_start(
                            out=swTr[:].rearrange("p (a o) -> p a o", o=P)[:, jj * 8:(jj + 1) * 8, :],
                            in_=wffnT.rearrange("(a p) o -> p a o", p=P)[:, jj * 8:(jj + 1) * 8, oc * P:(oc + 1) * P]),
                            wait=[tsg_prev] if jj == 0 else [], inc="s_w", incby=16)
                    tsg_prev = em.op("scalar", lambda e, s, par=par: e.activation(
                        swTs[par][:], swTr[:], AF.Sign),
                        wait=[lw, mmk.get((oc - 2, 1))], inc="s_a")
                    for th in range(2):
                        step = 2 * oc + th
                        pp = th

                        def mms(e, s, par=par, pp=pp, th=th):
                            r = None
                            for ic in range(IC):
                                r = e.matmul(ps[pp][:],
                                             swTs[par][:, ic * P:(ic + 1) * P],
                                             sx_fm[:, ic * TOK + th * BLK: ic * TOK + th * BLK + BLK],
                                             start=(ic == 0), stop=(ic == IC - 1))
                            return r
                        tmm = em.op("tensor", mms,
                                    wait=[tsg_prev, sg_tok, evk.get(step - 2)], inc="s_mm")
                        mmk[(oc, th)] = tmm
                        tev = em.op("scalar", lambda e, s, pp=pp, oc=oc: e.activation(
                            stg[pp][:], ps[pp][:], AF.Silu,
                            bias=0.0, scale=scale_arr[:, oc:oc + 1]),
                            wait=[tmm, tsc, stk.get(step - 2)], inc="s_ev")
                        evk[step] = tev
                        stk[step] = em.op("sync", lambda e, s, oc=oc, th=th, pp=pp: e.dma_start(
                            out=g_out[oc * P:(oc + 1) * P, th * BLK:(th + 1) * BLK],
                            in_=stg[pp][:]), wait=[tev], inc="s_st", incby=16)
                em.flush(nc, sems)

        em.op("sync", lambda e, s: e.nop(),
              wait=[("s_st", em.cnt.get("s_st", 0)), ("s_vt", em.cnt.get("s_vt", 0))])
        em.flush(nc, sems)
    return nc




_ATT_DBG = 0


def build_launch_ATT(layer):
    DBG = _ATT_DBG
    Lp = L0 if layer == 0 else L1
    G = Lp["G"]
    C, sc = Lp["C"], Lp["scale"]
    R = 256 * G
    nheads = 2 if layer == 0 else 1
    NCH = R // P
    DEPTH = 3
    NKC = S // P

    nc = bass.Bass("TRN2", target_bir_lowering=False)
    qh = nc.dram_tensor("qh", [R, S], BF16, kind="ExternalInput").ap()
    kh = nc.dram_tensor("kh", [R, S], BF16, kind="ExternalInput").ap()
    vh = nc.dram_tensor("vh", [S, R], BF16, kind="ExternalInput").ap()
    mneg = nc.dram_tensor("mneg", [P, P], F32, kind="ExternalInput").ap()
    trim = nc.dram_tensor("trim", [P, P], BF16, kind="ExternalInput").ap()
    identd = nc.dram_tensor("identb", [P, P], BF16, kind="ExternalInput").ap()
    hseld = nc.dram_tensor("hsel", [2, 2 * P], F32, kind="ExternalInput").ap()
    if layer == 0:
        slp = nc.dram_tensor("slp", [P, 4], F32, kind="ExternalInput").ap()
        dmask = nc.dram_tensor("dmask", [P, P], F32, kind="ExternalInput").ap()
        iotac = nc.dram_tensor("iotac", [P, 1], F32, kind="ExternalInput").ap()
        kvt = nc.dram_tensor("kvt", [P, 4 * NKC], F32, kind="ExternalInput").ap()
    oh = nc.dram_tensor("oh", [R, S], BF16, kind="ExternalOutput").ap()

    es = ExitStack()
    with es:
        kres = es.enter_context(nc.sbuf_tensor("kres", [P, NCH * S], BF16))
        vres = es.enter_context(nc.sbuf_tensor("vres", [P, NKC * R], BF16))
        qsb = [es.enter_context(nc.sbuf_tensor(f"qsb{j}", [P, 2 * BLK], BF16))
               for j in range(2)]
        pt = [es.enter_context(nc.sbuf_tensor(f"pt{j}", [P, BLK], BF16))
              for j in range(DEPTH)]
        bias_bc = [[es.enter_context(nc.sbuf_tensor(f"bbc{h}_{j}", [P, BLK], BF16))
                    for j in range(2)] for h in range(nheads)]
        biasrow = es.enter_context(nc.sbuf_tensor("biasrow", [1, BLK], BF16))
        trim_sb = es.enter_context(nc.sbuf_tensor("trim_sb", [P, P], BF16))
        identb = es.enter_context(nc.sbuf_tensor("identb_sb", [P, P], BF16))
        mneg_sb = es.enter_context(nc.sbuf_tensor("mneg_sb", [P, P], F32))
        nth = 4 if layer == 0 else 1
        tile_h = [es.enter_context(nc.sbuf_tensor(f"tile_h{h}", [P, P], F32))
                  for h in range(nth)]
        sdiag = es.enter_context(nc.sbuf_tensor("sdiag", [P, P], F32))
        m0c = es.enter_context(nc.sbuf_tensor("m0c", [P, 1], F32))
        altc = es.enter_context(nc.sbuf_tensor("altc", [P, 1], F32))
        t1c = es.enter_context(nc.sbuf_tensor("t1c", [P, 1], F32))
        bcol = es.enter_context(nc.sbuf_tensor("bcol", [P, 8], BF16))
        km = es.enter_context(nc.sbuf_tensor("km", [P, 2], F32))
        kmb = es.enter_context(nc.sbuf_tensor("kmb", [P, 16], BF16))
        denf = es.enter_context(nc.sbuf_tensor("denf", [2, BLK], F32))
        recf = es.enter_context(nc.sbuf_tensor("recf", [P, BLK], F32))
        ostg = [es.enter_context(nc.sbuf_tensor(f"ostg{j}", [P, 2 * BLK], BF16))
                for j in range(2)]
        oneh = [es.enter_context(nc.sbuf_tensor(f"oneh{h}", [P, 2], BF16))
                for h in range(nheads)]
        hsel_sb = es.enter_context(nc.sbuf_tensor("hsel_sb", [2, 2 * P], F32))
        onesr_b = es.enter_context(nc.sbuf_tensor("onesr_b", [1, P], BF16))
        if layer == 0:
            slp_sb = es.enter_context(nc.sbuf_tensor("slp_sb", [P, 4], F32))
            dm_sb = es.enter_context(nc.sbuf_tensor("dm_sb", [P, P], F32))
            iot_sb = es.enter_context(nc.sbuf_tensor("iot_sb", [P, 1], F32))
            kvt_sb = es.enter_context(nc.sbuf_tensor("kvt_sb", [P, 4 * NKC], F32))
            r60 = es.enter_context(nc.sbuf_tensor("r60", [P, 4], F32))

        ps_s = [es.enter_context(nc.psum_tensor(f"ps_s{j}", [P, BLK], F32))
                for j in range(DEPTH)]
        ps_o = [es.enter_context(nc.psum_tensor(f"ps_o{j}", [P, BLK], F32))
                for j in range(2)]
        ps_d = es.enter_context(nc.psum_tensor("ps_d", [2, BLK], F32))
        ps_p1 = es.enter_context(nc.psum_tensor("ps_p1", [P, P], F32))
        ps_p1b = es.enter_context(nc.psum_tensor("ps_p1b", [P, BLK], F32))

        sems = make_sems(nc, es, ["s_ld", "s_w", "s_q", "s_mm", "s_a", "s_v",
                                  "s_st", "s_fin", "s_g"])
        em = Em()

        consts = [(trim_sb, trim), (identb, identd), (mneg_sb, mneg),
                  (hsel_sb, hseld)]
        if layer == 0:
            consts += [(slp_sb, slp), (dm_sb, dmask), (iot_sb, iotac),
                       (kvt_sb, kvt)]
        for buf, dram in consts:
            em.op("sync", lambda e, s, buf=buf, dram=dram: e.dma_start(
                out=buf[:], in_=dram), inc="s_ld", incby=16)
        cst = em.tok("s_ld")
        em.op("gpsimd", lambda e, s: e.memset(kmb[:], 0.0))
        for h in range(nheads):
            em.op("gpsimd", lambda e, s, h=h: e.memset(oneh[h][:], 0.0))
            em.op("gpsimd", lambda e, s, h=h: e.memset(oneh[h][:, h:h + 1], 1.0))
        gcst = em.op("gpsimd", lambda e, s: e.memset(onesr_b[:], 1.0), inc="s_g")
        for c in range(NCH):
            em.op("gpsimd", lambda e, s, c=c: e.dma_start(
                out=kres[:, c * S:(c + 1) * S], in_=kh[c * P:(c + 1) * P, :]),
                inc="s_w", incby=16)
        for t0 in range(0, NKC, 8):
            em.op("gpsimd", lambda e, s, t0=t0: e.dma_start(
                out=vres[:].rearrange("p (t r) -> p t r", r=R)[:, t0:t0 + 8, :],
                in_=vh.rearrange("(t p) r -> p t r", p=P)[:, t0:t0 + 8, :]),
                inc="s_w", incby=16)
        kv_tok = em.tok("s_w")
        if layer == 0:
            for h in range(4):
                em.op("vector", lambda e, s, h=h: e.tensor_scalar(
                    tile_h[h][:], dm_sb[:], slp_sb[:, h:h + 1], None, ALU.mult),
                    wait=[cst])
                em.op("vector", lambda e, s, h=h: e.tensor_tensor(
                    tile_h[h][:], tile_h[h][:], mneg_sb[:], ALU.add))
            em.op("vector", lambda e, s: e.tensor_scalar(
                r60[:], slp_sb[:], 127.0, -60.0, ALU.mult, ALU.add))
        else:
            em.op("vector", lambda e, s: e.tensor_copy(tile_h[0][:], mneg_sb[:]),
                  wait=[cst])
        th_tok = em.op("vector", lambda e, s: e.nop(), inc="s_v")

        tasks = [(g, B) for g in range(G) for B in range(NBLK)]
        if DBG & 8:
            tasks = tasks[:1]
        NT_ = len(tasks)
        tq_tok = {}
        bias_tok = {}
        qsb_free = {}
        bbc_free = {}
        p1_free = [None]
        p1b_free = [None]
        ofin_tok = {}
        ost_tok = {}
        exp_toks = {}
        pv_toks = {}
        gstep = [0]

        def emit_pass1_A(idx):
            g, B = tasks[idx]
            par = idx % 2
            q0 = B * BLK
            tq_tok[idx] = em.op("sync", lambda e, s, g=g, B=B, par=par: e.dma_start(
                out=qsb[par][:].rearrange("p (a t) -> p a t", a=2),
                in_=qh[g * 256:(g + 1) * 256, B * BLK:(B + 1) * BLK]
                .rearrange("(a p) t -> p a t", p=P)),
                wait=[qsb_free.get(idx)], inc="s_q", incby=16)
            if DBG & 1:
                for hh in range(nheads):
                    bias_tok[(idx, hh)] = em.op(
                        "vector", lambda e, s, hh=hh, par=par: e.memset(
                            bias_bc[hh][par][:], 0.0),
                        wait=[bbc_free.get((hh, idx))], inc="s_v")
                return
            for j in range(2 if layer == 1 else nheads):
                c = (g * 2 + j) if layer == 0 else j
                em.op("vector", lambda e, s, j=j, c=c, q0=q0: e.reduce_sum(
                    km[:, j:j + 1], kres[:, c * S + q0: c * S + q0 + BLK], AX.X),
                    wait=[kv_tok, p1_free[0]])
                em.op("vector", lambda e, s, j=j: e.tensor_scalar_mul(
                    km[:, j:j + 1], km[:, j:j + 1], 1.0 / BLK))
            tkmb = None
            for j in range(2 if layer == 1 else nheads):
                tkmb = em.op("vector", lambda e, s, j=j: e.tensor_copy(
                    kmb[:, j * 8:j * 8 + 1], km[:, j:j + 1]), inc="s_v")
            prev_stt = None
            for hh in range(nheads):
                hsl = g * 2 + hh
                th = tile_h[hsl] if layer == 0 else tile_h[0]
                for sdx in range(4):
                    ktc = q0 + sdx * P

                    def dsmm(e, s, hh=hh, sdx=sdx, ktc=ktc, g=g, par=par):
                        if layer == 0:
                            c = g * 2 + hh
                            return e.matmul(
                                ps_p1[:, 0:P],
                                qsb[par][:, hh * BLK + sdx * P: hh * BLK + (sdx + 1) * P],
                                kres[:, c * S + ktc: c * S + ktc + P],
                                start=True, stop=True)
                        e.matmul(ps_p1[:, 0:P],
                                 qsb[par][:, sdx * P:(sdx + 1) * P],
                                 kres[:, ktc: ktc + P], start=True, stop=False)
                        return e.matmul(
                            ps_p1[:, 0:P],
                            qsb[par][:, BLK + sdx * P: BLK + (sdx + 1) * P],
                            kres[:, S + ktc: S + ktc + P], start=False, stop=True)
                    tdq = em.op("tensor", dsmm,
                                wait=[tq_tok[idx], kv_tok, prev_stt, p1_free[0]],
                                inc="s_mm")

                    def altm(e, s, hh=hh, sdx=sdx, par=par):
                        if layer == 0:
                            return e.matmul(
                                ps_p1b[:, 0:8],
                                qsb[par][:, hh * BLK + sdx * P: hh * BLK + (sdx + 1) * P],
                                kmb[:, hh * 8:(hh + 1) * 8], start=True, stop=True)
                        e.matmul(ps_p1b[:, 0:8],
                                 qsb[par][:, sdx * P:(sdx + 1) * P],
                                 kmb[:, 0:8], start=True, stop=False)
                        return e.matmul(
                            ps_p1b[:, 0:8],
                            qsb[par][:, BLK + sdx * P: BLK + (sdx + 1) * P],
                            kmb[:, 8:16], start=False, stop=True)
                    talt = None
                    if not (DBG & 64):
                        talt = em.op("tensor", altm,
                                     wait=[tkmb, prev_stt, p1b_free[0]],
                                     inc="s_mm")
                    tst = em.op("vector", lambda e, s, th=th: e.scalar_tensor_tensor(
                        out=sdiag[:], in0=ps_p1[:, 0:P], scalar=sc, in1=th[:],
                        op0=ALU.mult, op1=ALU.add), wait=[tdq, th_tok])
                    em.op("vector", lambda e, s: e.tensor_reduce(
                        m0c[:, 0:1], sdiag[:], AX.X, ALU.max))
                    if DBG & 64:
                        em.op("vector", lambda e, s: e.memset(altc[:, 0:1], -1e30))
                    else:
                        p1b_free[0] = em.op("vector", lambda e, s: e.tensor_scalar(
                            altc[:, 0:1], ps_p1b[:, 0:1],
                            sc, C, ALU.mult, ALU.add), wait=[talt], inc="s_v")
                    em.op("vector", lambda e, s: e.tensor_scalar_add(
                        t1c[:, 0:1], m0c[:, 0:1], 80.0))
                    em.op("vector", lambda e, s: e.tensor_tensor(
                        m0c[:, 0:1], m0c[:, 0:1], altc[:, 0:1], ALU.max))
                    if layer == 0 or B == 0:
                        em.op("vector", lambda e, s: e.tensor_tensor(
                            m0c[:, 0:1], m0c[:, 0:1], t1c[:, 0:1], ALU.min))
                    if layer == 0:
                        em.op("vector", lambda e, s, hsl=hsl: e.tensor_tensor(
                            m0c[:, 0:1], m0c[:, 0:1], r60[:, hsl:hsl + 1],
                            ALU.max))
                        em.op("vector", lambda e, s, q0=q0, sdx=sdx: e.tensor_scalar_add(
                            t1c[:, 0:1], iot_sb[:, 0:1], float(q0 + sdx * P)))
                        em.op("vector", lambda e, s, hsl=hsl: e.tensor_tensor(
                            t1c[:, 0:1], t1c[:, 0:1], slp_sb[:, hsl:hsl + 1],
                            ALU.mult))
                        em.op("vector", lambda e, s: e.tensor_tensor(
                            m0c[:, 0:1], m0c[:, 0:1], t1c[:, 0:1], ALU.add))
                        prev_stt = em.op("vector", lambda e, s, hh=hh, sdx=sdx: e.tensor_scalar_mul(
                            bcol[:, hh * 4 + sdx: hh * 4 + sdx + 1], m0c[:, 0:1],
                            -1.0 / sc), inc="s_v")
                    else:
                        prev_stt = em.op("vector", lambda e, s, hh=hh, sdx=sdx: e.tensor_scalar_mul(
                            bcol[:, hh * 4 + sdx: hh * 4 + sdx + 1], m0c[:, 0:1],
                            -1.0), inc="s_v")
            p1_free[0] = prev_stt

        def emit_pass1_B(idx):
            if DBG & 1:
                return
            g, B = tasks[idx]
            par = idx % 2
            if DBG & 32:
                for hh in range(nheads):
                    bias_tok[(idx, hh)] = em.op(
                        "vector", lambda e, s, hh=hh, par=par: e.memset(
                            bias_bc[hh][par][:], 0.0),
                        wait=[bbc_free.get((hh, idx)), p1_free[0]], inc="s_v")
                return
            bchain = p1_free[0]
            for hh in range(nheads):
                tpc = None
                for sdx in range(4):
                    tpc = em.op("tensor", lambda e, s, hh=hh, sdx=sdx: e.matmul(
                        ps_p1b[0:1, sdx * P:(sdx + 1) * P],
                        bcol[:, hh * 4 + sdx: hh * 4 + sdx + 1],
                        identb[:], start=True, stop=True),
                        wait=[bchain, cst, p1b_free[0]], inc="s_mm")
                trc = em.op("scalar", lambda e, s: e.activation(
                    biasrow[0:1, :], ps_p1b[0:1, 0:BLK], AF.Identity),
                    wait=[tpc], inc="s_a")
                tbc = em.op("tensor", lambda e, s: e.matmul(
                    ps_p1b[:, 0:BLK], onesr_b[0:1, :], biasrow[0:1, :],
                    start=True, stop=True), wait=[trc, gcst], inc="s_mm")
                p1b_free[0] = bias_tok[(idx, hh)] = em.op(
                    "vector", lambda e, s, hh=hh, par=par: e.tensor_copy(
                        bias_bc[hh][par][:], ps_p1b[:, 0:BLK]),
                    wait=[tbc, bbc_free.get((hh, idx))], inc="s_v")

        def emit_sweep(idx):
            g, B = tasks[idx]
            par = idx % 2
            injA = idx + 1 if idx + 1 < NT_ else None
            injB = injA
            last_smm = None
            for si, sb in enumerate(range(B, -1, -1)):
                for ci in range(3, -1, -1):
                    in_blk = (sb == B)
                    ktc_abs = sb * 4 + ci
                    for hh in range(nheads):
                        hsl = g * 2 + hh
                        step = gstep[0]
                        jp = step % DEPTH
                        first = (si == 0 and ci == 3)
                        last = (sb == 0 and ci == 0)

                        def smm(e, s, jp=jp, hh=hh, par=par, ktc_abs=ktc_abs, g=g):
                            e.matmul(ps_s[jp][:], identb[:], bias_bc[hh][par][:],
                                     start=True, stop=False)
                            if layer == 0:
                                c = g * 2 + hh
                                return e.matmul(
                                    ps_s[jp][:],
                                    kres[:, c * S + ktc_abs * P: c * S + (ktc_abs + 1) * P],
                                    qsb[par][:, hh * BLK:(hh + 1) * BLK],
                                    start=False, stop=True)
                            e.matmul(ps_s[jp][:],
                                     kres[:, ktc_abs * P:(ktc_abs + 1) * P],
                                     qsb[par][:, 0:BLK], start=False, stop=False)
                            return e.matmul(
                                ps_s[jp][:],
                                kres[:, S + ktc_abs * P: S + (ktc_abs + 1) * P],
                                qsb[par][:, BLK:2 * BLK], start=False, stop=True)
                        waits = [bias_tok[(idx, hh)], tq_tok[idx], kv_tok,
                                 exp_toks.get(step - DEPTH)]
                        if first:
                            waits += ofin_tok.get(idx - 1, [])
                        tsm = em.op("tensor", smm, wait=waits, inc="s_mm")
                        last_smm = tsm

                        def expf(e, s, jp=jp, hsl=hsl, ktc_abs=ktc_abs):
                            if layer == 0:
                                col = hsl * NKC + ktc_abs
                                return e.activation(pt[jp][:], ps_s[jp][:],
                                                    AF.Exp,
                                                    bias=kvt_sb[:, col:col + 1],
                                                    scale=sc)
                            return e.activation(pt[jp][:], ps_s[jp][:], AF.Exp,
                                                bias=0.0, scale=1.0)
                        texp = em.op("scalar", expf, wait=[tsm], inc="s_a")
                        exp_toks[step] = texp
                        pready = texp
                        if in_blk:
                            if ci > 0:
                                em.op("vector", lambda e, s, jp=jp, ci=ci: e.memset(
                                    pt[jp][:, 0:ci * P], 0.0), wait=[texp])
                            pready = em.op("vector", lambda e, s, jp=jp, ci=ci: e.tensor_tensor(
                                pt[jp][:, ci * P:(ci + 1) * P],
                                pt[jp][:, ci * P:(ci + 1) * P], trim_sb[:],
                                ALU.mult), wait=[texp], inc="s_v")

                        def dpv(e, s, jp=jp, hh=hh, ktc_abs=ktc_abs, g=g,
                                first=first, last=last):
                            nd = 2 if layer == 0 else 1
                            if not (DBG & 4):
                                e.matmul(ps_d[0:nd, :], oneh[hh][:, 0:nd],
                                         pt[jp][:],
                                         start=(first and hh == 0),
                                         stop=(last and hh == nheads - 1))
                            if layer == 0:
                                return e.matmul(
                                    ps_o[hh][:],
                                    vres[:, ktc_abs * R + g * 256 + hh * P:
                                         ktc_abs * R + g * 256 + (hh + 1) * P],
                                    pt[jp][:], start=first, stop=last)
                            e.matmul(ps_o[0][:], vres[:, ktc_abs * R: ktc_abs * R + P],
                                     pt[jp][:], start=first, stop=last)
                            return e.matmul(
                                ps_o[1][:], vres[:, ktc_abs * R + P: ktc_abs * R + 2 * P],
                                pt[jp][:], start=first, stop=last)
                        tpv = em.op("tensor", dpv, wait=[pready], inc="s_mm")
                        pv_toks[step] = tpv
                        gstep[0] += 1
                if si == 0 and injA is not None:
                    emit_pass1_A(injA)
                elif si == 1 and injB is not None:
                    emit_pass1_B(injB)
                    injB = None
            if injB is not None:
                emit_pass1_B(injB)
            qsb_free[idx + 2] = last_smm
            for hh in range(nheads):
                bbc_free[(hh, idx + 2)] = last_smm

            nd = 2 if layer == 0 else 1
            if DBG & 2:
                fins = []
                for hh in range(nheads):
                    for j in range(2 if layer == 1 else 1):
                        slot = hh if layer == 0 else j
                        po = ps_o[hh] if layer == 0 else ps_o[j]
                        tfin = em.op("vector", lambda e, s, slot=slot, po=po, par=par: e.tensor_copy(
                            ostg[par][:, slot * BLK:(slot + 1) * BLK], po[:]),
                            wait=[pv_toks[gstep[0] - 1],
                                  ost_tok.get((idx - 2, slot))], inc="s_fin")
                        fins.append(tfin)
                        ost_tok[(idx, slot)] = em.op(
                            "sync", lambda e, s, slot=slot, g=g, B=B, par=par: e.dma_start(
                                out=oh[g * 256 + slot * P: g * 256 + (slot + 1) * P,
                                       B * BLK:(B + 1) * BLK],
                                in_=ostg[par][:, slot * BLK:(slot + 1) * BLK]),
                            wait=[tfin], inc="s_st", incby=16)
                ofin_tok[idx] = fins
                return
            tdnf = em.op("scalar", lambda e, s, nd=nd: e.activation(
                denf[0:nd, :], ps_d[0:nd, :], AF.Identity),
                wait=[pv_toks[gstep[0] - 1]], inc="s_a")
            fins = []
            prev_rec = None
            for hh in range(nheads):
                tbc2 = em.op("tensor", lambda e, s, hh=hh, nd=nd: e.matmul(
                    ps_p1b[:, 0:BLK], hsel_sb[0:nd, hh * P:(hh + 1) * P],
                    denf[0:nd, :], start=True, stop=True),
                    wait=[tdnf, p1b_free[0], prev_rec], inc="s_mm")
                trec = em.op("vector", lambda e, s: e.reciprocal(
                    recf[:], ps_p1b[:, 0:BLK]), wait=[tbc2], inc="s_v")
                prev_rec = trec
                p1b_free[0] = trec
                for j in range(2 if layer == 1 else 1):
                    slot = hh if layer == 0 else j
                    po = ps_o[hh] if layer == 0 else ps_o[j]
                    tfin = em.op("vector", lambda e, s, slot=slot, po=po, par=par: e.tensor_tensor(
                        ostg[par][:, slot * BLK:(slot + 1) * BLK], po[:],
                        recf[:], ALU.mult),
                        wait=[ost_tok.get((idx - 2, slot))], inc="s_fin")
                    fins.append(tfin)
                    ost_tok[(idx, slot)] = em.op(
                        "sync", lambda e, s, slot=slot, g=g, B=B, par=par: e.dma_start(
                            out=oh[g * 256 + slot * P: g * 256 + (slot + 1) * P,
                                   B * BLK:(B + 1) * BLK],
                            in_=ostg[par][:, slot * BLK:(slot + 1) * BLK]),
                        wait=[tfin], inc="s_st", incby=16)
            ofin_tok[idx] = fins + [tdnf]

        emit_pass1_A(0)
        emit_pass1_B(0)
        for idx in range(NT_):
            emit_sweep(idx)
        em.op("sync", lambda e, s: e.nop(),
              wait=[("s_st", em.cnt.get("s_st", 0))])
        em.flush(nc, sems)
    return nc



def build_launch_PROJ(layer):
    Lp = L0 if layer == 0 else L1
    hd = Lp["hd"]
    nic = hd // P
    nc = bass.Bass("TRN2", target_bir_lowering=False)
    xsrc = nc.dram_tensor("xsrc", [TOK, D], F32, kind="ExternalInput").ap()
    ofm = nc.dram_tensor("ofm", [hd, TOK], BF16, kind="ExternalInput").ap()
    gfm = nc.dram_tensor("gfm", [D, TOK], BF16, kind="ExternalInput").ap()
    wpro = nc.dram_tensor("wpro", [hd, D], F32, kind="ExternalInput").ap()
    bprr = nc.dram_tensor("bprr", [P, IC], F32, kind="ExternalInput").ap()
    xout = nc.dram_tensor("xout", [TOK, D], F32, kind="ExternalOutput").ap()
    dfm_dram = nc.dram_tensor("dfm_dram", [D, TOK], BF16, kind="Internal").ap()

    es = ExitStack()
    with es:
        o_sb = es.enter_context(nc.sbuf_tensor("o_sb", [P, nic * TOK], BF16))
        bprr_sb = es.enter_context(nc.sbuf_tensor("bprr_sb", [P, IC], F32))
        wbuf = [es.enter_context(nc.sbuf_tensor(f"wbuf{j}", [P, nic * P], BF16)) for j in range(2)]
        gbuf = [es.enter_context(nc.sbuf_tensor(f"gbuf{j}", [P, TOK], BF16)) for j in range(2)]
        dstg = [es.enter_context(nc.sbuf_tensor(f"dstg{j}", [P, BLK], BF16)) for j in range(2)]
        ps = [es.enter_context(nc.psum_tensor(f"ps{j}", [P, BLK], F32)) for j in range(2)]
        sems = make_sems(nc, es, ["s_ld", "s_w", "s_mm", "s_v", "s_st", "s_kv"])
        em = Em()

        tb = em.op("sync", lambda e, s: e.dma_start(out=bprr_sb[:], in_=bprr),
                   inc="s_ld", incby=16)
        to = None
        for jj in range(4):
            to = em.op("sync", lambda e, s, jj=jj: e.dma_start(
                out=o_sb[:].rearrange("p (a t) -> p a t", t=TOK)[:, jj * (nic // 4):(jj + 1) * (nic // 4), :],
                in_=ofm.rearrange("(a p) t -> p a t", p=P)[:, jj * (nic // 4):(jj + 1) * (nic // 4), :]),
                inc="s_ld", incby=16)

        pm, pe, pst = {}, {}, {}
        for oc in range(IC):
            par = oc % 2
            lw = None
            for jj in range(4):
                lw = em.op("gpsimd", lambda e, s, oc=oc, par=par, jj=jj: e.dma_start(
                    out=wbuf[par][:].rearrange("p (a o) -> p a o", o=P)[:, jj * (nic // 4):(jj + 1) * (nic // 4), :],
                    in_=wpro.rearrange("(a p) o -> p a o", p=P)[:, jj * (nic // 4):(jj + 1) * (nic // 4), oc * P:(oc + 1) * P]),
                    wait=[pm.get((oc - 2, 1))] if jj == 0 else [], inc="s_w", incby=16)
            tg = em.op("gpsimd", lambda e, s, oc=oc, par=par: e.dma_start(
                out=gbuf[par][:], in_=gfm[oc * P:(oc + 1) * P, :]),
                wait=[pe.get((oc - 2, 1))], inc="s_w", incby=16)
            for bi in range(2):
                pp = bi

                def pmm(e, s, par=par, bi=bi, pp=pp):
                    r = None
                    for ic in range(nic):
                        r = e.matmul(ps[pp][:], wbuf[par][:, ic * P:(ic + 1) * P],
                                     o_sb[:, ic * TOK + bi * BLK: ic * TOK + bi * BLK + BLK],
                                     start=(ic == 0), stop=(ic == nic - 1))
                    return r
                tpm = em.op("tensor", pmm, wait=[lw, to, pe.get((oc - 2, bi))], inc="s_mm")
                pm[(oc, bi)] = tpm
                tpe = em.op("vector", lambda e, s, oc=oc, bi=bi, pp=pp, par=par: e.scalar_tensor_tensor(
                    out=dstg[pp][:], in0=ps[pp][:], scalar=bprr_sb[:, oc:oc + 1],
                    in1=gbuf[par][:, bi * BLK:(bi + 1) * BLK],
                    op0=ALU.add, op1=ALU.mult),
                    wait=[tpm, tg, tb, pst.get((oc - 2, bi))], inc="s_v")
                pe[(oc, bi)] = tpe
                pst[(oc, bi)] = em.op("sync", lambda e, s, oc=oc, bi=bi, pp=pp: e.dma_start(
                    out=dfm_dram[oc * P:(oc + 1) * P, bi * BLK:(bi + 1) * BLK],
                    in_=dstg[pp][:]), wait=[tpe], inc="s_st", incby=16)
        dall = em.tok("s_st")

        s3 = ExitStack()
        with s3:
            dtok = [s3.enter_context(nc.sbuf_tensor(f"dtok{j}", [P, D], BF16)) for j in range(2)]
            xtile = [s3.enter_context(nc.sbuf_tensor(f"xtile{j}", [P, D], F32)) for j in range(2)]
            otile = [s3.enter_context(nc.sbuf_tensor(f"otile{j}", [P, D], F32)) for j in range(2)]
            rtk = {}
            for tt in range(NT):
                par = tt % 2
                ttr = em.op("sync", lambda e, s, tt=tt, par=par: e.dma_start_transpose(
                    dtok[par][:], dfm_dram[:, tt * P:(tt + 1) * P]),
                    wait=[dall, rtk.get(tt - 2)], inc="s_kv", incby=16)
                tx = em.op("sync", lambda e, s, tt=tt, par=par: e.dma_start(
                    out=xtile[par][:], in_=xsrc[tt * P:(tt + 1) * P, :]),
                    inc="s_kv", incby=16)
                tadd = em.op("vector", lambda e, s, par=par: e.tensor_tensor(
                    otile[par][:], xtile[par][:], dtok[par][:], ALU.add),
                    wait=[ttr, tx], inc="s_v")
                rtk[tt] = em.op("sync", lambda e, s, tt=tt, par=par: e.dma_start(
                    out=xout[tt * P:(tt + 1) * P, :], in_=otile[par][:]),
                    wait=[tadd], inc="s_st", incby=16)
            em.op("sync", lambda e, s: e.nop(), wait=[("s_st", em.cnt.get("s_st", 0))])
            em.flush(nc, sems)
    return nc



_PROGS = {}


def _get_prog(key, builder):
    if key not in _PROGS:
        _PROGS[key] = builder()
    return _PROGS[key]


def _shard_rows(full):
    return [np.ascontiguousarray(np.concatenate(
        [full[c * BLK:(c + 1) * BLK], full[(15 - c) * BLK:(16 - c) * BLK]], axis=0))
        for c in range(8)]


def _unshard_rows(shards):
    full = np.empty((S,) + shards[0].shape[1:], dtype=shards[0].dtype)
    for c in range(8):
        full[c * BLK:(c + 1) * BLK] = shards[c][0:BLK]
        full[(15 - c) * BLK:(16 - c) * BLK] = shards[c][BLK:2 * BLK]
    return full


def _uncol_shards(mats):
    rows = mats[0].shape[0]
    full = np.empty((rows, S), dtype=mats[0].dtype)
    for c in range(8):
        full[:, c * BLK:(c + 1) * BLK] = mats[c][:, 0:BLK]
        full[:, (15 - c) * BLK:(16 - c) * BLK] = mats[c][:, BLK:2 * BLK]
    return full


def _col_shard(full):
    return [np.ascontiguousarray(np.concatenate(
        [full[:, c * BLK:(c + 1) * BLK], full[:, (15 - c) * BLK:(16 - c) * BLK]], axis=1))
        for c in range(8)]


def kernel(L0_attn_hidden_states, L0_attn_w_attn, L0_attn_b_attn, L0_attn_w_proj,
           L0_attn_b_proj, L0_ffn_weight, L1_attn_w_qkv, L1_attn_b_qkv,
           L1_attn_w_o, L1_attn_b_o, L1_ffn_A, L1_ffn_B, L1_ffn_Wr):
    f32 = np.float32
    x = np.ascontiguousarray(np.asarray(L0_attn_hidden_states, dtype=f32).reshape(S, D))
    x_sh = _shard_rows(x)

    w_attn = np.ascontiguousarray(np.asarray(L0_attn_w_attn, dtype=f32))
    barr0 = np.ascontiguousarray(np.asarray(L0_attn_b_attn, dtype=f32).reshape(96, P).T)
    w_proj = np.ascontiguousarray(np.asarray(L0_attn_w_proj, dtype=f32))
    bprr0 = np.ascontiguousarray(np.asarray(L0_attn_b_proj, dtype=f32).reshape(IC, P).T)
    wffn = np.ascontiguousarray(np.asarray(L0_ffn_weight, dtype=f32))
    wffnT = np.ascontiguousarray(wffn.T)
    wqkv1 = np.ascontiguousarray(np.asarray(L1_attn_w_qkv, dtype=f32).T)
    barr1 = np.ascontiguousarray(np.asarray(L1_attn_b_qkv, dtype=f32).reshape(48, P).T)
    wo1 = np.ascontiguousarray(np.asarray(L1_attn_w_o, dtype=f32).T)
    bprr1 = np.ascontiguousarray(np.asarray(L1_attn_b_o, dtype=f32).reshape(IC, P).T)
    A1 = np.asarray(L1_ffn_A, dtype=f32)
    B1 = np.asarray(L1_ffn_B, dtype=f32)
    Wr1 = np.asarray(L1_ffn_Wr, dtype=f32)
    aT = np.ascontiguousarray(A1.reshape(64, D).T)
    bP = np.ascontiguousarray(B1.transpose(0, 2, 1).reshape(64, D))
    wrT = np.ascontiguousarray(Wr1.T)
    oneh = np.zeros((4, 64), f32)
    for e in range(4):
        oneh[e, e * 16:(e + 1) * 16] = 1.0

    ii = np.arange(P, dtype=f32)
    iotac = np.ascontiguousarray(ii.reshape(P, 1))
    identb = np.ascontiguousarray(np.eye(P, dtype=ml_dtypes.bfloat16))
    hsel = np.zeros((2, 2 * P), f32)
    hsel[0, 0:P] = 1.0
    hsel[1, P:2 * P] = 1.0
    dmask = np.ascontiguousarray((ii[None, :] - ii[:, None]).astype(f32))
    mneg = np.ascontiguousarray(np.where(ii[None, :] > ii[:, None], NEG, 0.0).astype(f32))
    trim = np.ascontiguousarray(
        np.where(ii[:, None] <= ii[None, :], 1.0, 0.0).astype(ml_dtypes.bfloat16))

    cores = list(range(8))

    def run_A(layer, src_shards):
        ncp = _get_prog(("A", layer), lambda: build_launch_A(layer))
        maps = []
        for c in cores:
            m = {"src": src_shards[c]}
            if layer == 0:
                m.update(wqkv=w_attn, barr=barr0, wffn=wffn, wffnT=wffnT)
            else:
                m.update(wqkv=wqkv1, barr=barr1, wrT=wrT, aT=aT, bP=bP, oneh=oneh)
            maps.append(m)
        return run_bass_kernel_spmd(ncp, maps, core_ids=cores).results

    def run_ATT(layer, resA):
        Lp = L0 if layer == 0 else L1
        R = 256 * Lp["G"]
        Qabs = _uncol_shards([r["q_out"] for r in resA])
        Kabs = _uncol_shards([r["k_out"] for r in resA])
        Vsh = [r["v_out"] for r in resA]
        Vabs_sh = _unshard_rows(Vsh)
        ncp = _get_prog(("ATT", layer), lambda: build_launch_ATT(layer))
        maps = []
        kbase = np.arange(64, dtype=f32)[:, None] * P + np.arange(P, dtype=f32)[None, :]
        for c in cores:
            m = dict(
                qh=np.ascontiguousarray(Qabs[c * R:(c + 1) * R, :]),
                kh=np.ascontiguousarray(Kabs[c * R:(c + 1) * R, :]),
                vh=np.ascontiguousarray(Vabs_sh[:, c * R:(c + 1) * R]),
                mneg=mneg, trim=trim, identb=identb, hsel=hsel)
            if layer == 0:
                slp = np.ascontiguousarray(
                    np.tile(SLOPES[4 * c:4 * c + 4][None, :], (P, 1)).astype(f32))
                kvt = np.empty((P, 4 * 64), f32)
                for h in range(4):
                    kvt[:, h * 64:(h + 1) * 64] = (SLOPES[4 * c + h] * kbase).T
                m.update(slp=slp, dmask=dmask, iotac=iotac,
                         kvt=np.ascontiguousarray(kvt))
            maps.append(m)
        res = run_bass_kernel_spmd(ncp, maps, core_ids=cores).results
        Ofull = np.concatenate([r["oh"] for r in res], axis=0)
        return _col_shard(Ofull)

    def run_PROJ(layer, src_shards, o_shards, resA):
        ncp = _get_prog(("PROJ", layer), lambda: build_launch_PROJ(layer))
        maps = [dict(xsrc=src_shards[c], ofm=o_shards[c], gfm=resA[c]["g_out"],
                     wpro=(w_proj if layer == 0 else wo1),
                     bprr=(bprr0 if layer == 0 else bprr1)) for c in cores]
        res = run_bass_kernel_spmd(ncp, maps, core_ids=cores).results
        return [r["xout"] for r in res]

    resA0 = run_A(0, x_sh)
    o0_sh = run_ATT(0, resA0)
    x1_sh = run_PROJ(0, x_sh, o0_sh, resA0)
    resA1 = run_A(1, x1_sh)
    o1_sh = run_ATT(1, resA1)
    out_sh = run_PROJ(1, x1_sh, o1_sh, resA1)

    return _unshard_rows(out_sh).reshape(1, S, D).astype(np.float32)


# version 05
# speedup vs baseline: 1.4551x; 1.4551x over previous
import numpy as np
import ml_dtypes
from contextlib import ExitStack
import concourse.bass as bass
import concourse.mybir as mybir
from concourse.bass_utils import run_bass_kernel_spmd

F32 = mybir.dt.float32
BF16 = mybir.dt.bfloat16
AF = mybir.ActivationFunctionType
ALU = mybir.AluOpType
AX = mybir.AxisListType

P = 128
TOK = 1024
NT = TOK // P
D = 4096
IC = D // P
S = 8192
BLK = 512
NBLK = S // BLK
NEG = -1.0e30

L0 = dict(H=32, dh=128, hd=4096, oqk=12288, scale=1.0 / 128.0, alibi=True, C=20.0, G=2)
L1 = dict(H=8, dh=256, hd=2048, oqk=6144, scale=1.0, alibi=False, C=90.0, G=1)

ENGS = ("sync", "tensor", "vector", "scalar", "gpsimd")


def alibi_slopes(n):
    import math
    cp2 = 2 ** math.floor(math.log2(n))
    base = 2 ** (-2 ** (-(math.log2(cp2) - 3)))
    sl = base ** np.arange(1, cp2 + 1, dtype=np.float32)
    if cp2 != n:
        eb = 2 ** (-2 ** (-(math.log2(2 * cp2) - 3)))
        n_rem = min(cp2, n - cp2)
        extra = eb ** np.arange(1, 1 + 2 * n_rem, 2, dtype=np.float32)
        sl = np.concatenate([sl, extra])
    return sl.astype(np.float32)


SLOPES = alibi_slopes(32)


class Em:

    def __init__(self):
        self.ops = []
        self.cnt = {}
        self.lastwait = {e: {} for e in ENGS}

    def tok(self, sem):
        return (sem, self.cnt.get(sem, 0))

    AUTO = {"vector": "s_v", "scalar": "s_a"}

    def op(self, eng, fn, wait=(), inc=None, incby=1):
        if eng in self.AUTO:
            auto = self.AUTO[eng]
            wait = list(wait) + [(auto, self.cnt.get(auto, 0))]
            inc, incby = auto, 1
        waits = []
        for wt in wait:
            if wt is None:
                continue
            sem, c = wt
            if c <= 0 or self.lastwait[eng].get(sem, -1) >= c:
                continue
            self.lastwait[eng][sem] = c
            waits.append((sem, c))
        ret = None
        if inc is not None:
            self.cnt[inc] = self.cnt.get(inc, 0) + incby
            ret = (inc, self.cnt[inc])

        def run(engine, sems, fn=fn, waits=waits, inc=inc, incby=incby):
            for s0, c0 in waits:
                engine.wait_ge(sems[s0], c0)
            r = fn(engine, sems)
            if inc is not None:
                r.then_inc(sems[inc], incby)

        self.ops.append((eng, run))
        return ret

    def flush(self, nc, sems):
        if not self.ops:
            return
        bes = ExitStack()
        with bes:
            block = bes.enter_context(nc.Block())
            for name in ENGS:
                ops = [fn for e, fn in self.ops if e == name]
                if not ops:
                    continue
                reg = {"sync": block.sync, "tensor": block.tensor,
                       "vector": block.vector, "scalar": block.scalar,
                       "gpsimd": block.gpsimd}[name]

                def body(engine, ops=ops):
                    for fn in ops:
                        fn(engine, sems)
                reg(body)
        self.ops = []


def make_sems(nc, es, names):
    return {n: es.enter_context(nc.semaphore(n)) for n in names}



def emit_ln(em, src, xn_dram, xn_fm, bufs):
    xt, xc, sqs, xnt, mu, vs, rstd, eps = bufs
    teps = em.op("gpsimd", lambda e, s: e.memset(eps[:, 0:1], 1e-5), inc="s_g")
    last_store = None
    for t in range(NT):
        tl = em.op("sync", lambda e, s, t=t: e.dma_start(
            out=xt[:], in_=src[t * P:(t + 1) * P, :]),
            wait=[last_store], inc="s_ld", incby=16)
        em.op("vector", lambda e, s: e.reduce_sum(mu[:, 0:1], xt[:], AX.X), wait=[tl])
        em.op("vector", lambda e, s: e.tensor_scalar_mul(mu[:, 0:1], mu[:, 0:1], 1.0 / D))
        txc = em.op("vector", lambda e, s: e.tensor_scalar(
            xc[:], xt[:], mu[:, 0:1], None, ALU.subtract), inc="s_v")
        tsq = em.op("scalar", lambda e, s: e.activation(
            sqs[:], xc[:], AF.Square), wait=[txc], inc="s_a")
        tv = em.op("vector", lambda e, s: e.reduce_sum(
            vs[:, 0:1], sqs[:], AX.X), wait=[tsq], inc="s_v")
        em.op("scalar", lambda e, s: e.activation(
            rstd[:, 0:1], vs[:, 0:1], AF.Ln, bias=eps[:, 0:1], scale=1.0 / D),
            wait=[tv, teps])
        ta = em.op("scalar", lambda e, s: e.activation(
            rstd[:, 0:1], rstd[:, 0:1], AF.Exp, bias=0.0, scale=-0.5), inc="s_a")
        tx = em.op("vector", lambda e, s: e.tensor_scalar(
            xnt[:], xc[:], rstd[:, 0:1], None, ALU.mult), wait=[ta], inc="s_v")
        last_store = em.op("sync", lambda e, s, t=t: e.dma_start(
            out=xn_dram[t * P:(t + 1) * P, :], in_=xnt[:]), wait=[tx], inc="s_st", incby=16)
    fm_tok = None
    if xn_fm is not None:
        for ic in range(IC):
            fm_tok = em.op("sync", lambda e, s, ic=ic: e.dma_start_transpose(
                xn_fm[:, ic * TOK:(ic + 1) * TOK], xn_dram[:, ic * P:(ic + 1) * P]),
                wait=[last_store], inc="s_fm", incby=16)
    return last_store, fm_tok


def emit_qkv(em, Lp, wqkv_dram, barr_sb, barr_tok, xn_fm, fm_tok, wbuf, stg, ps,
             q_out, k_out, vfm_dram):
    hd, oqk = Lp["hd"], Lp["oqk"]
    noc, nQ = oqk // P, hd // P
    mm_toks, ev_toks, st_toks = {}, {}, {}
    for oc in range(noc):
        par = oc % 2
        lastw = None
        for jj in range(4):
            lastw = em.op("gpsimd", lambda e, s, oc=oc, par=par, jj=jj: e.dma_start(
                out=wbuf[par][:].rearrange("p (a o) -> p a o", o=P)[:, jj * 8:(jj + 1) * 8, :],
                in_=wqkv_dram.rearrange("(a p) o -> p a o", p=P)[:, jj * 8:(jj + 1) * 8, oc * P:(oc + 1) * P]),
                wait=[mm_toks.get((oc - 2, 1))] if jj == 0 else [], inc="s_w", incby=16)
        for th in range(2):
            step = 2 * oc + th
            pp = th

            def mms(e, s, par=par, pp=pp, th=th):
                r = None
                for ic in range(IC):
                    r = e.matmul(ps[pp][:],
                                 wbuf[par][:, ic * P:(ic + 1) * P],
                                 xn_fm[:, ic * TOK + th * BLK: ic * TOK + th * BLK + BLK],
                                 start=(ic == 0), stop=(ic == IC - 1))
                return r
            tmm = em.op("tensor", mms, wait=[lastw, fm_tok, ev_toks.get(step - 2)], inc="s_mm")
            mm_toks[(oc, th)] = tmm
            tev = em.op("scalar", lambda e, s, pp=pp, oc=oc: e.activation(
                stg[pp][:], ps[pp][:], AF.Identity, bias=barr_sb[:, oc:oc + 1], scale=1.0),
                wait=[tmm, st_toks.get(step - 2), barr_tok], inc="s_ev")
            ev_toks[step] = tev
            if oc < nQ:
                dst, row = q_out, oc * P
            elif oc < 2 * nQ:
                dst, row = k_out, (oc - nQ) * P
            else:
                dst, row = vfm_dram, (oc - 2 * nQ) * P
            st_toks[step] = em.op("sync", lambda e, s, dst=dst, row=row, th=th, pp=pp: e.dma_start(
                out=dst[row:row + P, th * BLK:(th + 1) * BLK], in_=stg[pp][:]),
                wait=[tev], inc="s_st", incby=16)
    return st_toks[2 * noc - 1], ev_toks[2 * noc - 1], ev_toks[2 * noc - 2]


def emit_v_transpose(em, Lp, vfm_dram, v_out, vstg, vdone_tok):
    last = {}
    for tt in range(NT):
        par = tt % 2
        ttr = em.op("sync", lambda e, s, par=par, tt=tt: e.dma_start_transpose(
            vstg[par][:], vfm_dram[:, tt * P:(tt + 1) * P]),
            wait=[vdone_tok, last.get(tt - 2)], inc="s_vt", incby=16)
        last[tt] = em.op("sync", lambda e, s, par=par, tt=tt: e.dma_start(
            out=v_out[tt * P:(tt + 1) * P, :], in_=vstg[par][:]),
            wait=[ttr], inc="s_vt", incby=16)
    return last[NT - 1]


def emit_moe(nc, em, scope, xn_fm, fm_tok, wrT, aT, bP, oneh, g_out, ps, psum_free):
    wr_sb = scope.enter_context(nc.sbuf_tensor("wr_sb", [P, IC * 4], BF16))
    aT_sb = scope.enter_context(nc.sbuf_tensor("aT_sb", [P, IC * 64], BF16))
    bP_sb = scope.enter_context(nc.sbuf_tensor("bP_sb", [64, D], BF16))
    oneh_sb = scope.enter_context(nc.sbuf_tensor("oneh_sb", [4, 64], F32))
    ident = scope.enter_context(nc.sbuf_tensor("identM", [P, P], F32))
    lg = scope.enter_context(nc.sbuf_tensor("lg", [P, 4], F32))
    pmin = scope.enter_context(nc.sbuf_tensor("pmin", [P, 8], F32))
    nm1 = scope.enter_context(nc.sbuf_tensor("nm1", [P, 1], F32))
    t2s = scope.enter_context(nc.sbuf_tensor("t2s", [P, 1], F32))
    ex4 = scope.enter_context(nc.sbuf_tensor("ex4", [P, 4], F32))
    ind4 = scope.enter_context(nc.sbuf_tensor("ind4", [P, 4], F32))
    den = scope.enter_context(nc.sbuf_tensor("den", [P, 1], F32))
    rec = scope.enter_context(nc.sbuf_tensor("rec", [P, 1], F32))
    g4 = scope.enter_context(nc.sbuf_tensor("g4", [P, 4], F32))
    gates_fm = scope.enter_context(nc.sbuf_tensor("gates_fm", [4, TOK], F32))
    gbc_sb = scope.enter_context(nc.sbuf_tensor("gbc_sb", [64, BLK], F32))
    smid = scope.enter_context(nc.sbuf_tensor("smid", [64, BLK], BF16))
    gstg = [scope.enter_context(nc.sbuf_tensor(f"gstg{j}", [P, BLK], BF16)) for j in range(2)]
    ps_l = scope.enter_context(nc.psum_tensor("ps_l", [P, BLK], F32))
    ps_t = scope.enter_context(nc.psum_tensor("ps_t", [P, BLK], F32))
    ps_m = ps[0][0:64, :]
    ps_g = ps[1][0:64, :]

    t1 = em.op("gpsimd", lambda e, s: e.dma_start(
        out=wr_sb[:].rearrange("p (a r) -> p a r", r=4),
        in_=wrT.rearrange("(a p) r -> p a r", p=P)), inc="s_w", incby=16)
    t2_ = em.op("gpsimd", lambda e, s: e.dma_start(
        out=aT_sb[:].rearrange("p (a r) -> p a r", r=64),
        in_=aT.rearrange("(a p) r -> p a r", p=P)), inc="s_w", incby=16)
    t3 = em.op("gpsimd", lambda e, s: e.dma_start(out=bP_sb[:], in_=bP), inc="s_w", incby=16)
    t4 = em.op("sync", lambda e, s: e.dma_start(out=oneh_sb[:], in_=oneh), inc="s_ld", incby=16)
    tms = em.op("gpsimd", lambda e, s: e.memset(ident[:], 0.0), inc="s_g")
    ti = em.op("gpsimd", lambda e, s: e.affine_select(
        out=ident[:], in_=ident[:], compare_op=ALU.not_equal, fill=1.0,
        base=0, pattern=[[-1, P]], channel_multiplier=1), wait=[tms], inc="s_g")

    pairs = [(0, 1), (0, 2), (0, 3), (1, 2), (1, 3), (2, 3)]
    gtok = {}
    prev_tr = None
    for t in range(NT):
        def lmm(e, s, t=t):
            r = None
            for ic in range(IC):
                r = e.matmul(ps_l[:, 0:4],
                             xn_fm[:, ic * TOK + t * P: ic * TOK + (t + 1) * P],
                             wr_sb[:, ic * 4:(ic + 1) * 4],
                             start=(ic == 0), stop=(ic == IC - 1))
            return r
        tml = em.op("tensor", lmm, wait=[t1, fm_tok, gtok.get(t - 1)], inc="s_mm")
        em.op("vector", lambda e, s: e.tensor_copy(lg[:], ps_l[:, 0:4]), wait=[tml])
        for i, (a, b) in enumerate(pairs):
            em.op("vector", lambda e, s, i=i, a=a, b=b: e.tensor_tensor(
                pmin[:, i:i + 1], lg[:, a:a + 1], lg[:, b:b + 1], ALU.min))
        em.op("vector", lambda e, s: e.tensor_tensor(pmin[:, 6:7], pmin[:, 0:1], pmin[:, 1:2], ALU.max))
        for i in (2, 3, 4):
            em.op("vector", lambda e, s, i=i: e.tensor_tensor(
                pmin[:, 6:7], pmin[:, 6:7], pmin[:, i:i + 1], ALU.max))
        em.op("vector", lambda e, s: e.tensor_tensor(t2s[:, 0:1], pmin[:, 6:7], pmin[:, 5:6], ALU.max))
        em.op("vector", lambda e, s: e.tensor_reduce(nm1[:, 0:1], lg[:], AX.X, ALU.max, negate=True))
        tind = em.op("vector", lambda e, s: e.tensor_scalar(
            ind4[:], lg[:], t2s[:, 0:1], None, ALU.is_ge), inc="s_v")
        tex = em.op("scalar", lambda e, s: e.activation(
            ex4[:], lg[:], AF.Exp, bias=nm1[:, 0:1], scale=1.0), wait=[tind], inc="s_a")
        em.op("vector", lambda e, s: e.tensor_tensor(g4[:], ind4[:], ex4[:], ALU.mult), wait=[tex])
        em.op("vector", lambda e, s: e.reduce_sum(den[:, 0:1], g4[:], AX.X))
        em.op("vector", lambda e, s: e.reciprocal(rec[:, 0:1], den[:, 0:1]))
        tg4 = em.op("vector", lambda e, s: e.tensor_scalar(
            g4[:], g4[:], rec[:, 0:1], None, ALU.mult), inc="s_v")
        ttr = em.op("tensor", lambda e, s: e.transpose(ps_t[0:4, 0:P], g4[:], ident[:]),
                    wait=[tg4, ti, prev_tr], inc="s_mm")
        prev_tr = gtok[t] = em.op("vector", lambda e, s, t=t: e.tensor_copy(
            gates_fm[:, t * P:(t + 1) * P], ps_t[0:4, 0:P]), wait=[ttr], inc="s_v")

    dguard = {}
    last = None
    for th in range(2):
        def midmm(e, s, th=th):
            r = None
            for ic in range(IC):
                r = e.matmul(ps_m, aT_sb[:, ic * 64:(ic + 1) * 64],
                             xn_fm[:, ic * TOK + th * BLK: ic * TOK + th * BLK + BLK],
                             start=(ic == 0), stop=(ic == IC - 1))
            return r
        tmid = em.op("tensor", midmm, wait=[t2_, gtok[NT - 1]] + psum_free, inc="s_mm")
        tgb = em.op("tensor", lambda e, s, th=th: e.matmul(
            ps_g, oneh_sb[:], gates_fm[:, th * BLK:(th + 1) * BLK],
            start=True, stop=True), wait=[t4], inc="s_mm")
        em.op("vector", lambda e, s: e.tensor_copy(gbc_sb[:], ps_g),
              wait=[tgb, dguard.get(("sm", th - 1))])
        tsm = em.op("vector", lambda e, s: e.scalar_tensor_tensor(
            out=smid[:], in0=ps_m, scalar=4.0, in1=gbc_sb[:],
            op0=ALU.mult, op1=ALU.mult), wait=[tmid], inc="s_v")
        dguard[("sm", th)] = tsm
        for oc in range(IC):
            pp = oc % 2
            pdst = ps_l if pp == 0 else ps_t
            tdm = em.op("tensor", lambda e, s, oc=oc, pdst=pdst: e.matmul(
                pdst[:], bP_sb[:, oc * P:(oc + 1) * P], smid[:],
                start=True, stop=True),
                wait=[tsm, t3, dguard.get((th, oc - 2))], inc="s_mm")
            tev = em.op("scalar", lambda e, s, pdst=pdst, pp=pp: e.activation(
                gstg[pp][:], pdst[:], AF.Silu),
                wait=[tdm, dguard.get(("st", th, oc - 2))], inc="s_a")
            dguard[(th, oc)] = tev
            last = em.op("sync", lambda e, s, oc=oc, th=th, pp=pp: e.dma_start(
                out=g_out[oc * P:(oc + 1) * P, th * BLK:(th + 1) * BLK],
                in_=gstg[pp][:]), wait=[tev], inc="s_st", incby=16)
            dguard[("st", th, oc)] = last
    return last


def build_launch_A(layer, debug=False):
    Lp = L0 if layer == 0 else L1
    nc = bass.Bass("TRN2", target_bir_lowering=False)
    src = nc.dram_tensor("src", [TOK, D], F32, kind="ExternalInput").ap()
    hd, oqk = Lp["hd"], Lp["oqk"]
    noc = oqk // P
    wqkv = nc.dram_tensor("wqkv", [D, oqk], F32, kind="ExternalInput").ap()
    barr = nc.dram_tensor("barr", [P, noc], F32, kind="ExternalInput").ap()
    if layer == 0:
        wffn = nc.dram_tensor("wffn", [D, D], F32, kind="ExternalInput").ap()
        wffnT = nc.dram_tensor("wffnT", [D, D], F32, kind="ExternalInput").ap()
    else:
        wrT = nc.dram_tensor("wrT", [D, 4], F32, kind="ExternalInput").ap()
        aT = nc.dram_tensor("aT", [D, 64], F32, kind="ExternalInput").ap()
        bP = nc.dram_tensor("bP", [64, D], F32, kind="ExternalInput").ap()
        oneh = nc.dram_tensor("oneh", [4, 64], F32, kind="ExternalInput").ap()

    q_out = nc.dram_tensor("q_out", [hd, TOK], BF16, kind="ExternalOutput").ap()
    k_out = nc.dram_tensor("k_out", [hd, TOK], BF16, kind="ExternalOutput").ap()
    v_out = nc.dram_tensor("v_out", [TOK, hd], BF16, kind="ExternalOutput").ap()
    g_out = nc.dram_tensor("g_out", [D, TOK], BF16, kind="ExternalOutput").ap()
    xn_dram = nc.dram_tensor("xn_dram", [TOK, D], BF16, kind="ExternalOutput" if debug else "Internal").ap()
    vfm_dram = nc.dram_tensor("vfm_dram", [hd, TOK], BF16, kind="Internal").ap()

    es = ExitStack()
    with es:
        wbuf = [es.enter_context(nc.sbuf_tensor(f"wbuf{j}", [P, IC * P], BF16)) for j in range(2)]
        stg = [es.enter_context(nc.sbuf_tensor(f"stg{j}", [P, BLK], BF16)) for j in range(2)]
        vstg = [es.enter_context(nc.sbuf_tensor(f"vstg{j}", [P, hd], BF16)) for j in range(2)]
        barr_sb = es.enter_context(nc.sbuf_tensor("barr_sb", [P, noc], F32))
        ps = [es.enter_context(nc.psum_tensor(f"ps{j}", [P, BLK], F32)) for j in range(2)]
        sems = make_sems(nc, es, ["s_ld", "s_v", "s_a", "s_st", "s_fm", "s_w",
                                  "s_mm", "s_ev", "s_vt", "s_g"])
        em = Em()
        barr_tok = em.op("sync", lambda e, s: e.dma_start(out=barr_sb[:], in_=barr),
                         inc="s_ld", incby=16)

        s1 = ExitStack()
        with s1:
            xn_fm = s1.enter_context(nc.sbuf_tensor("xn_fm", [P, IC * TOK], BF16))
            xt = s1.enter_context(nc.sbuf_tensor("xt", [P, D], F32))
            xc = s1.enter_context(nc.sbuf_tensor("xc", [P, D], F32))
            sqs = s1.enter_context(nc.sbuf_tensor("sqs", [P, D], F32))
            xnt = s1.enter_context(nc.sbuf_tensor("xnt", [P, D], BF16))
            mu = s1.enter_context(nc.sbuf_tensor("mu", [P, 1], F32))
            vs = s1.enter_context(nc.sbuf_tensor("vs", [P, 1], F32))
            rstd = s1.enter_context(nc.sbuf_tensor("rstd", [P, 1], F32))
            epsb = s1.enter_context(nc.sbuf_tensor("epsb", [P, 1], F32))

            _, fm_tok = emit_ln(em, src, xn_dram, xn_fm, (xt, xc, sqs, xnt, mu, vs, rstd, epsb))
            vlast, evl, evl2 = emit_qkv(em, Lp, wqkv, barr_sb, barr_tok, xn_fm, fm_tok,
                                        wbuf, stg, ps, q_out, k_out, vfm_dram)
            emit_v_transpose(em, Lp, vfm_dram, v_out, vstg, vlast)
            if layer == 1:
                emit_moe(nc, em, s1, xn_fm, fm_tok, wrT, aT, bP, oneh, g_out, ps,
                         [evl, evl2])
            em.flush(nc, sems)

        if layer == 0:
            s2 = ExitStack()
            with s2:
                sx_fm = s2.enter_context(nc.sbuf_tensor("sx_fm", [P, IC * TOK], BF16))
                fmchunk = s2.enter_context(nc.sbuf_tensor("fmchunk", [P, TOK], BF16))
                wnat = s2.enter_context(nc.sbuf_tensor("wnat", [P, D], BF16))
                snat = s2.enter_context(nc.sbuf_tensor("snat", [P, D], BF16))
                absout = s2.enter_context(nc.sbuf_tensor("absout", [P, D], BF16))
                swTr = s2.enter_context(nc.sbuf_tensor("swTr", [P, IC * P], BF16))
                swTs = [s2.enter_context(nc.sbuf_tensor(f"swTs{j}", [P, IC * P], BF16)) for j in range(2)]
                scale_arr = s2.enter_context(nc.sbuf_tensor("scale_arr", [P, IC], F32))
                asum = s2.enter_context(nc.sbuf_tensor("asum", [P, 1], F32))

                sg_tok = None
                for ic in range(IC):
                    tt = em.op("sync", lambda e, s, ic=ic: e.dma_start_transpose(
                        fmchunk[:], xn_dram[:, ic * P:(ic + 1) * P]),
                        wait=[sg_tok], inc="s_fm", incby=16)
                    sg_tok = em.op("scalar", lambda e, s, ic=ic: e.activation(
                        sx_fm[:, ic * TOK:(ic + 1) * TOK], fmchunk[:], AF.Sign),
                        wait=[tt], inc="s_a")

                mmk, evk, stk = {}, {}, {}
                tabs_prev, tsg_prev = None, None
                for oc in range(IC):
                    par = oc % 2
                    tn = em.op("gpsimd", lambda e, s, oc=oc: e.dma_start(
                        out=wnat[:], in_=wffn[oc * P:(oc + 1) * P, :]),
                        wait=[tabs_prev], inc="s_w", incby=16)
                    tsn = em.op("scalar", lambda e, s: e.activation(
                        absout[:], wnat[:], AF.Abs), wait=[tn], inc="s_a")
                    tabs_prev = em.op("vector", lambda e, s: e.reduce_sum(
                        asum[:, 0:1], absout[:], AX.X), wait=[tsn], inc="s_v")
                    tsc = em.op("vector", lambda e, s, oc=oc: e.tensor_scalar_mul(
                        scale_arr[:, oc:oc + 1], asum[:, 0:1], 1.0 / D), inc="s_g")
                    lw = None
                    for jj in range(4):
                        lw = em.op("gpsimd", lambda e, s, oc=oc, jj=jj: e.dma_start(
                            out=swTr[:].rearrange("p (a o) -> p a o", o=P)[:, jj * 8:(jj + 1) * 8, :],
                            in_=wffnT.rearrange("(a p) o -> p a o", p=P)[:, jj * 8:(jj + 1) * 8, oc * P:(oc + 1) * P]),
                            wait=[tsg_prev] if jj == 0 else [], inc="s_w", incby=16)
                    tsg_prev = em.op("scalar", lambda e, s, par=par: e.activation(
                        swTs[par][:], swTr[:], AF.Sign),
                        wait=[lw, mmk.get((oc - 2, 1))], inc="s_a")
                    for th in range(2):
                        step = 2 * oc + th
                        pp = th

                        def mms(e, s, par=par, pp=pp, th=th):
                            r = None
                            for ic in range(IC):
                                r = e.matmul(ps[pp][:],
                                             swTs[par][:, ic * P:(ic + 1) * P],
                                             sx_fm[:, ic * TOK + th * BLK: ic * TOK + th * BLK + BLK],
                                             start=(ic == 0), stop=(ic == IC - 1))
                            return r
                        tmm = em.op("tensor", mms,
                                    wait=[tsg_prev, sg_tok, evk.get(step - 2)], inc="s_mm")
                        mmk[(oc, th)] = tmm
                        tev = em.op("scalar", lambda e, s, pp=pp, oc=oc: e.activation(
                            stg[pp][:], ps[pp][:], AF.Silu,
                            bias=0.0, scale=scale_arr[:, oc:oc + 1]),
                            wait=[tmm, tsc, stk.get(step - 2)], inc="s_ev")
                        evk[step] = tev
                        stk[step] = em.op("sync", lambda e, s, oc=oc, th=th, pp=pp: e.dma_start(
                            out=g_out[oc * P:(oc + 1) * P, th * BLK:(th + 1) * BLK],
                            in_=stg[pp][:]), wait=[tev], inc="s_st", incby=16)
                em.flush(nc, sems)

        em.op("sync", lambda e, s: e.nop(),
              wait=[("s_st", em.cnt.get("s_st", 0)), ("s_vt", em.cnt.get("s_vt", 0))])
        em.flush(nc, sems)
    return nc




_ATT_DBG = 0


def build_launch_ATT(layer):
    DBG = _ATT_DBG
    Lp = L0 if layer == 0 else L1
    G = Lp["G"]
    C, sc = Lp["C"], Lp["scale"]
    R = 256 * G
    nheads = 2 if layer == 0 else 1
    NCH = R // P
    DEPTH = 3
    NKC = S // P

    nc = bass.Bass("TRN2", target_bir_lowering=False)
    qh = nc.dram_tensor("qh", [R, S], BF16, kind="ExternalInput").ap()
    kh = nc.dram_tensor("kh", [R, S], BF16, kind="ExternalInput").ap()
    vh = nc.dram_tensor("vh", [S, R], BF16, kind="ExternalInput").ap()
    mneg = nc.dram_tensor("mneg", [P, P], F32, kind="ExternalInput").ap()
    trim = nc.dram_tensor("trim", [P, P], BF16, kind="ExternalInput").ap()
    identd = nc.dram_tensor("identb", [P, P], BF16, kind="ExternalInput").ap()
    hseld = nc.dram_tensor("hsel", [2, 2 * P], F32, kind="ExternalInput").ap()
    if layer == 0:
        slp = nc.dram_tensor("slp", [P, 4], F32, kind="ExternalInput").ap()
        dmask = nc.dram_tensor("dmask", [P, P], F32, kind="ExternalInput").ap()
        iotac = nc.dram_tensor("iotac", [P, 1], F32, kind="ExternalInput").ap()
        kvt = nc.dram_tensor("kvt", [P, 4 * NKC], F32, kind="ExternalInput").ap()
    oh = nc.dram_tensor("oh", [R, S], BF16, kind="ExternalOutput").ap()

    es = ExitStack()
    with es:
        kres = es.enter_context(nc.sbuf_tensor("kres", [P, NCH * S], BF16))
        vres = es.enter_context(nc.sbuf_tensor("vres", [P, NKC * R], BF16))
        qsb = [es.enter_context(nc.sbuf_tensor(f"qsb{j}", [P, 2 * BLK], BF16))
               for j in range(2)]
        pt = [es.enter_context(nc.sbuf_tensor(f"pt{j}", [P, BLK], BF16))
              for j in range(DEPTH)]
        bias_bc = [[es.enter_context(nc.sbuf_tensor(f"bbc{h}_{j}", [P, BLK], BF16))
                    for j in range(2)] for h in range(nheads)]
        biasrow = es.enter_context(nc.sbuf_tensor("biasrow", [1, BLK], BF16))
        trim_sb = es.enter_context(nc.sbuf_tensor("trim_sb", [P, P], BF16))
        identb = es.enter_context(nc.sbuf_tensor("identb_sb", [P, P], BF16))
        mneg_sb = es.enter_context(nc.sbuf_tensor("mneg_sb", [P, P], F32))
        nth = 4 if layer == 0 else 1
        tile_h = [es.enter_context(nc.sbuf_tensor(f"tile_h{h}", [P, P], F32))
                  for h in range(nth)]
        sdiag = es.enter_context(nc.sbuf_tensor("sdiag", [P, P], F32))
        m0c = es.enter_context(nc.sbuf_tensor("m0c", [P, 1], F32))
        altc = es.enter_context(nc.sbuf_tensor("altc", [P, 1], F32))
        t1c = es.enter_context(nc.sbuf_tensor("t1c", [P, 1], F32))
        bcol = es.enter_context(nc.sbuf_tensor("bcol", [P, 8], BF16))
        km = es.enter_context(nc.sbuf_tensor("km", [P, 2], F32))
        kmb = es.enter_context(nc.sbuf_tensor("kmb", [P, 16], BF16))
        denf = es.enter_context(nc.sbuf_tensor("denf", [2, BLK], F32))
        recf = es.enter_context(nc.sbuf_tensor("recf", [P, BLK], F32))
        ostg = [es.enter_context(nc.sbuf_tensor(f"ostg{j}", [P, 2 * BLK], BF16))
                for j in range(2)]
        oneh = [es.enter_context(nc.sbuf_tensor(f"oneh{h}", [P, 2], BF16))
                for h in range(nheads)]
        hsel_sb = es.enter_context(nc.sbuf_tensor("hsel_sb", [2, 2 * P], F32))
        onesr_b = es.enter_context(nc.sbuf_tensor("onesr_b", [1, P], BF16))
        if layer == 0:
            slp_sb = es.enter_context(nc.sbuf_tensor("slp_sb", [P, 4], F32))
            dm_sb = es.enter_context(nc.sbuf_tensor("dm_sb", [P, P], F32))
            iot_sb = es.enter_context(nc.sbuf_tensor("iot_sb", [P, 1], F32))
            kvt_sb = es.enter_context(nc.sbuf_tensor("kvt_sb", [P, 4 * NKC], F32))
            r60 = es.enter_context(nc.sbuf_tensor("r60", [P, 4], F32))

        ps_s = [es.enter_context(nc.psum_tensor(f"ps_s{j}", [P, BLK], F32))
                for j in range(DEPTH)]
        ps_o = [es.enter_context(nc.psum_tensor(f"ps_o{j}", [P, BLK], F32))
                for j in range(2)]
        ps_d = es.enter_context(nc.psum_tensor("ps_d", [2, BLK], F32))
        ps_p1 = es.enter_context(nc.psum_tensor("ps_p1", [P, P], F32))
        ps_p1b = es.enter_context(nc.psum_tensor("ps_p1b", [P, BLK], F32))

        sems = make_sems(nc, es, ["s_ld", "s_w", "s_q", "s_mm", "s_a", "s_v",
                                  "s_st", "s_fin", "s_g"])
        em = Em()

        consts = [(trim_sb, trim), (identb, identd), (mneg_sb, mneg),
                  (hsel_sb, hseld)]
        if layer == 0:
            consts += [(slp_sb, slp), (dm_sb, dmask), (iot_sb, iotac),
                       (kvt_sb, kvt)]
        for buf, dram in consts:
            em.op("sync", lambda e, s, buf=buf, dram=dram: e.dma_start(
                out=buf[:], in_=dram), inc="s_ld", incby=16)
        cst = em.tok("s_ld")
        em.op("gpsimd", lambda e, s: e.memset(kmb[:], 0.0))
        for h in range(nheads):
            em.op("gpsimd", lambda e, s, h=h: e.memset(oneh[h][:], 0.0))
            em.op("gpsimd", lambda e, s, h=h: e.memset(oneh[h][:, h:h + 1], 1.0))
        gcst = em.op("gpsimd", lambda e, s: e.memset(onesr_b[:], 1.0), inc="s_g")
        for c in range(NCH):
            em.op("gpsimd", lambda e, s, c=c: e.dma_start(
                out=kres[:, c * S:(c + 1) * S], in_=kh[c * P:(c + 1) * P, :]),
                inc="s_w", incby=16)
        for t0 in range(0, NKC, 8):
            em.op("gpsimd", lambda e, s, t0=t0: e.dma_start(
                out=vres[:].rearrange("p (t r) -> p t r", r=R)[:, t0:t0 + 8, :],
                in_=vh.rearrange("(t p) r -> p t r", p=P)[:, t0:t0 + 8, :]),
                inc="s_w", incby=16)
        kv_tok = em.tok("s_w")
        if layer == 0:
            for h in range(4):
                em.op("vector", lambda e, s, h=h: e.tensor_scalar(
                    tile_h[h][:], dm_sb[:], slp_sb[:, h:h + 1], None, ALU.mult),
                    wait=[cst])
                em.op("vector", lambda e, s, h=h: e.tensor_tensor(
                    tile_h[h][:], tile_h[h][:], mneg_sb[:], ALU.add))
            em.op("vector", lambda e, s: e.tensor_scalar(
                r60[:], slp_sb[:], 127.0, -60.0, ALU.mult, ALU.add))
        else:
            em.op("vector", lambda e, s: e.tensor_copy(tile_h[0][:], mneg_sb[:]),
                  wait=[cst])
        th_tok = em.op("vector", lambda e, s: e.nop(), inc="s_v")

        tasks = [(g, B) for g in range(G) for B in range(NBLK)]
        if DBG & 8:
            tasks = tasks[:1]
        NT_ = len(tasks)
        tq_tok = {}
        bias_tok = {}
        qsb_free = {}
        bbc_free = {}
        p1_free = [None]
        p1b_free = [None]
        ofin_tok = {}
        ost_tok = {}
        exp_toks = {}
        pv_toks = {}
        gstep = [0]

        def emit_pass1_A(idx):
            g, B = tasks[idx]
            par = idx % 2
            q0 = B * BLK
            tq_tok[idx] = em.op("sync", lambda e, s, g=g, B=B, par=par: e.dma_start(
                out=qsb[par][:].rearrange("p (a t) -> p a t", a=2),
                in_=qh[g * 256:(g + 1) * 256, B * BLK:(B + 1) * BLK]
                .rearrange("(a p) t -> p a t", p=P)),
                wait=[qsb_free.get(idx)], inc="s_q", incby=16)
            if DBG & 1:
                for hh in range(nheads):
                    bias_tok[(idx, hh)] = em.op(
                        "vector", lambda e, s, hh=hh, par=par: e.memset(
                            bias_bc[hh][par][:], 0.0),
                        wait=[bbc_free.get((hh, idx))], inc="s_v")
                return
            for j in range(2 if layer == 1 else nheads):
                c = (g * 2 + j) if layer == 0 else j
                em.op("vector", lambda e, s, j=j, c=c, q0=q0: e.reduce_sum(
                    km[:, j:j + 1], kres[:, c * S + q0: c * S + q0 + BLK], AX.X),
                    wait=[kv_tok, p1_free[0]])
                em.op("vector", lambda e, s, j=j: e.tensor_scalar_mul(
                    km[:, j:j + 1], km[:, j:j + 1], 1.0 / BLK))
            tkmb = None
            for j in range(2 if layer == 1 else nheads):
                tkmb = em.op("vector", lambda e, s, j=j: e.tensor_copy(
                    kmb[:, j * 8:j * 8 + 1], km[:, j:j + 1]), inc="s_v")
            prev_stt = None
            for hh in range(nheads):
                hsl = g * 2 + hh
                th = tile_h[hsl] if layer == 0 else tile_h[0]
                for sdx in range(4):
                    ktc = q0 + sdx * P

                    def dsmm(e, s, hh=hh, sdx=sdx, ktc=ktc, g=g, par=par):
                        if layer == 0:
                            c = g * 2 + hh
                            return e.matmul(
                                ps_p1[:, 0:P],
                                qsb[par][:, hh * BLK + sdx * P: hh * BLK + (sdx + 1) * P],
                                kres[:, c * S + ktc: c * S + ktc + P],
                                start=True, stop=True)
                        e.matmul(ps_p1[:, 0:P],
                                 qsb[par][:, sdx * P:(sdx + 1) * P],
                                 kres[:, ktc: ktc + P], start=True, stop=False)
                        return e.matmul(
                            ps_p1[:, 0:P],
                            qsb[par][:, BLK + sdx * P: BLK + (sdx + 1) * P],
                            kres[:, S + ktc: S + ktc + P], start=False, stop=True)
                    tdq = em.op("tensor", dsmm,
                                wait=[tq_tok[idx], kv_tok, prev_stt, p1_free[0]],
                                inc="s_mm")

                    def altm(e, s, hh=hh, sdx=sdx, par=par):
                        if layer == 0:
                            return e.matmul(
                                ps_p1b[:, 0:8],
                                qsb[par][:, hh * BLK + sdx * P: hh * BLK + (sdx + 1) * P],
                                kmb[:, hh * 8:(hh + 1) * 8], start=True, stop=True)
                        e.matmul(ps_p1b[:, 0:8],
                                 qsb[par][:, sdx * P:(sdx + 1) * P],
                                 kmb[:, 0:8], start=True, stop=False)
                        return e.matmul(
                            ps_p1b[:, 0:8],
                            qsb[par][:, BLK + sdx * P: BLK + (sdx + 1) * P],
                            kmb[:, 8:16], start=False, stop=True)
                    talt = None
                    if not (DBG & 64):
                        talt = em.op("tensor", altm,
                                     wait=[tkmb, prev_stt, p1b_free[0]],
                                     inc="s_mm")
                    tst = em.op("vector", lambda e, s, th=th: e.scalar_tensor_tensor(
                        out=sdiag[:], in0=ps_p1[:, 0:P], scalar=sc, in1=th[:],
                        op0=ALU.mult, op1=ALU.add), wait=[tdq, th_tok])
                    em.op("vector", lambda e, s: e.tensor_reduce(
                        m0c[:, 0:1], sdiag[:], AX.X, ALU.max))
                    if DBG & 64:
                        em.op("vector", lambda e, s: e.memset(altc[:, 0:1], -1e30))
                    else:
                        p1b_free[0] = em.op("vector", lambda e, s: e.tensor_scalar(
                            altc[:, 0:1], ps_p1b[:, 0:1],
                            sc, C, ALU.mult, ALU.add), wait=[talt], inc="s_v")
                    em.op("vector", lambda e, s: e.tensor_scalar_add(
                        t1c[:, 0:1], m0c[:, 0:1], 80.0))
                    em.op("vector", lambda e, s: e.tensor_tensor(
                        m0c[:, 0:1], m0c[:, 0:1], altc[:, 0:1], ALU.max))
                    if layer == 0 or B == 0:
                        em.op("vector", lambda e, s: e.tensor_tensor(
                            m0c[:, 0:1], m0c[:, 0:1], t1c[:, 0:1], ALU.min))
                    if layer == 0:
                        em.op("vector", lambda e, s, hsl=hsl: e.tensor_tensor(
                            m0c[:, 0:1], m0c[:, 0:1], r60[:, hsl:hsl + 1],
                            ALU.max))
                        em.op("vector", lambda e, s, q0=q0, sdx=sdx: e.tensor_scalar_add(
                            t1c[:, 0:1], iot_sb[:, 0:1], float(q0 + sdx * P)))
                        em.op("vector", lambda e, s, hsl=hsl: e.tensor_tensor(
                            t1c[:, 0:1], t1c[:, 0:1], slp_sb[:, hsl:hsl + 1],
                            ALU.mult))
                        em.op("vector", lambda e, s: e.tensor_tensor(
                            m0c[:, 0:1], m0c[:, 0:1], t1c[:, 0:1], ALU.add))
                        prev_stt = em.op("vector", lambda e, s, hh=hh, sdx=sdx: e.tensor_scalar_mul(
                            bcol[:, hh * 4 + sdx: hh * 4 + sdx + 1], m0c[:, 0:1],
                            -1.0 / sc), inc="s_v")
                    else:
                        prev_stt = em.op("vector", lambda e, s, hh=hh, sdx=sdx: e.tensor_scalar_mul(
                            bcol[:, hh * 4 + sdx: hh * 4 + sdx + 1], m0c[:, 0:1],
                            -1.0), inc="s_v")
            p1_free[0] = prev_stt

        def emit_pass1_B(idx):
            if DBG & 1:
                return
            g, B = tasks[idx]
            par = idx % 2
            if DBG & 32:
                for hh in range(nheads):
                    bias_tok[(idx, hh)] = em.op(
                        "vector", lambda e, s, hh=hh, par=par: e.memset(
                            bias_bc[hh][par][:], 0.0),
                        wait=[bbc_free.get((hh, idx)), p1_free[0]], inc="s_v")
                return
            bchain = p1_free[0]
            for hh in range(nheads):
                tpc = None
                for sdx in range(4):
                    tpc = em.op("tensor", lambda e, s, hh=hh, sdx=sdx: e.matmul(
                        ps_p1b[0:1, sdx * P:(sdx + 1) * P],
                        bcol[:, hh * 4 + sdx: hh * 4 + sdx + 1],
                        identb[:], start=True, stop=True),
                        wait=[bchain, cst, p1b_free[0]], inc="s_mm")
                trc = em.op("scalar", lambda e, s: e.activation(
                    biasrow[0:1, :], ps_p1b[0:1, 0:BLK], AF.Identity),
                    wait=[tpc], inc="s_a")
                tbc = em.op("tensor", lambda e, s: e.matmul(
                    ps_p1b[:, 0:BLK], onesr_b[0:1, :], biasrow[0:1, :],
                    start=True, stop=True), wait=[trc, gcst], inc="s_mm")
                p1b_free[0] = bias_tok[(idx, hh)] = em.op(
                    "vector", lambda e, s, hh=hh, par=par: e.tensor_copy(
                        bias_bc[hh][par][:], ps_p1b[:, 0:BLK]),
                    wait=[tbc, bbc_free.get((hh, idx))], inc="s_v")

        def emit_sweep(idx):
            g, B = tasks[idx]
            par = idx % 2
            injA = idx + 1 if idx + 1 < NT_ else None
            injB = injA
            last_smm = None
            pend_dpv = [None]

            def flush_dpv():
                if pend_dpv[0] is not None:
                    fn, w, st = pend_dpv[0]
                    pv_toks[st] = em.op("tensor", fn, wait=w, inc="s_mm")
                    pend_dpv[0] = None
            for si, sb in enumerate(range(B, -1, -1)):
                for ci in range(3, -1, -1):
                    in_blk = (sb == B)
                    ktc_abs = sb * 4 + ci
                    for hh in range(nheads):
                        hsl = g * 2 + hh
                        step = gstep[0]
                        jp = step % DEPTH
                        first = (si == 0 and ci == 3)
                        last = (sb == 0 and ci == 0)

                        def smm(e, s, jp=jp, hh=hh, par=par, ktc_abs=ktc_abs, g=g):
                            e.matmul(ps_s[jp][:], identb[:], bias_bc[hh][par][:],
                                     start=True, stop=False)
                            if layer == 0:
                                c = g * 2 + hh
                                return e.matmul(
                                    ps_s[jp][:],
                                    kres[:, c * S + ktc_abs * P: c * S + (ktc_abs + 1) * P],
                                    qsb[par][:, hh * BLK:(hh + 1) * BLK],
                                    start=False, stop=True)
                            e.matmul(ps_s[jp][:],
                                     kres[:, ktc_abs * P:(ktc_abs + 1) * P],
                                     qsb[par][:, 0:BLK], start=False, stop=False)
                            return e.matmul(
                                ps_s[jp][:],
                                kres[:, S + ktc_abs * P: S + (ktc_abs + 1) * P],
                                qsb[par][:, BLK:2 * BLK], start=False, stop=True)
                        waits = [bias_tok[(idx, hh)], tq_tok[idx], kv_tok,
                                 exp_toks.get(step - DEPTH)]
                        if first:
                            waits += ofin_tok.get(idx - 1, [])
                        tsm = em.op("tensor", smm, wait=waits, inc="s_mm")
                        last_smm = tsm

                        def expf(e, s, jp=jp, hsl=hsl, ktc_abs=ktc_abs):
                            if layer == 0:
                                col = hsl * NKC + ktc_abs
                                return e.activation(pt[jp][:], ps_s[jp][:],
                                                    AF.Exp,
                                                    bias=kvt_sb[:, col:col + 1],
                                                    scale=sc)
                            return e.activation(pt[jp][:], ps_s[jp][:], AF.Exp,
                                                bias=0.0, scale=1.0)
                        texp = em.op("scalar", expf, wait=[tsm], inc="s_a")
                        exp_toks[step] = texp
                        pready = texp
                        if in_blk:
                            if ci > 0:
                                em.op("vector", lambda e, s, jp=jp, ci=ci: e.memset(
                                    pt[jp][:, 0:ci * P], 0.0), wait=[texp])
                            pready = em.op("vector", lambda e, s, jp=jp, ci=ci: e.tensor_tensor(
                                pt[jp][:, ci * P:(ci + 1) * P],
                                pt[jp][:, ci * P:(ci + 1) * P], trim_sb[:],
                                ALU.mult), wait=[texp], inc="s_v")

                        def dpv(e, s, jp=jp, hh=hh, ktc_abs=ktc_abs, g=g,
                                first=first, last=last):
                            nd = 2 if layer == 0 else 1
                            if not (DBG & 4):
                                e.matmul(ps_d[0:nd, :], oneh[hh][:, 0:nd],
                                         pt[jp][:],
                                         start=(first and hh == 0),
                                         stop=(last and hh == nheads - 1))
                            if layer == 0:
                                return e.matmul(
                                    ps_o[hh][:],
                                    vres[:, ktc_abs * R + g * 256 + hh * P:
                                         ktc_abs * R + g * 256 + (hh + 1) * P],
                                    pt[jp][:], start=first, stop=last)
                            e.matmul(ps_o[0][:], vres[:, ktc_abs * R: ktc_abs * R + P],
                                     pt[jp][:], start=first, stop=last)
                            return e.matmul(
                                ps_o[1][:], vres[:, ktc_abs * R + P: ktc_abs * R + 2 * P],
                                pt[jp][:], start=first, stop=last)
                        flush_dpv()
                        pend_dpv[0] = (dpv, [pready], step)
                        gstep[0] += 1
                if si == 0 and injA is not None:
                    emit_pass1_A(injA)
                elif si == 1 and injB is not None:
                    emit_pass1_B(injB)
                    injB = None
            flush_dpv()
            if injB is not None:
                emit_pass1_B(injB)
            qsb_free[idx + 2] = last_smm
            for hh in range(nheads):
                bbc_free[(hh, idx + 2)] = last_smm

            nd = 2 if layer == 0 else 1
            if DBG & 2:
                fins = []
                for hh in range(nheads):
                    for j in range(2 if layer == 1 else 1):
                        slot = hh if layer == 0 else j
                        po = ps_o[hh] if layer == 0 else ps_o[j]
                        tfin = em.op("vector", lambda e, s, slot=slot, po=po, par=par: e.tensor_copy(
                            ostg[par][:, slot * BLK:(slot + 1) * BLK], po[:]),
                            wait=[pv_toks[gstep[0] - 1],
                                  ost_tok.get((idx - 2, slot))], inc="s_fin")
                        fins.append(tfin)
                        ost_tok[(idx, slot)] = em.op(
                            "sync", lambda e, s, slot=slot, g=g, B=B, par=par: e.dma_start(
                                out=oh[g * 256 + slot * P: g * 256 + (slot + 1) * P,
                                       B * BLK:(B + 1) * BLK],
                                in_=ostg[par][:, slot * BLK:(slot + 1) * BLK]),
                            wait=[tfin], inc="s_st", incby=16)
                ofin_tok[idx] = fins
                return
            tdnf = em.op("scalar", lambda e, s, nd=nd: e.activation(
                denf[0:nd, :], ps_d[0:nd, :], AF.Identity),
                wait=[pv_toks[gstep[0] - 1]], inc="s_a")
            fins = []
            prev_rec = None
            for hh in range(nheads):
                tbc2 = em.op("tensor", lambda e, s, hh=hh, nd=nd: e.matmul(
                    ps_p1b[:, 0:BLK], hsel_sb[0:nd, hh * P:(hh + 1) * P],
                    denf[0:nd, :], start=True, stop=True),
                    wait=[tdnf, p1b_free[0], prev_rec], inc="s_mm")
                trec = em.op("vector", lambda e, s: e.reciprocal(
                    recf[:], ps_p1b[:, 0:BLK]), wait=[tbc2], inc="s_v")
                prev_rec = trec
                p1b_free[0] = trec
                for j in range(2 if layer == 1 else 1):
                    slot = hh if layer == 0 else j
                    po = ps_o[hh] if layer == 0 else ps_o[j]
                    tfin = em.op("vector", lambda e, s, slot=slot, po=po, par=par: e.tensor_tensor(
                        ostg[par][:, slot * BLK:(slot + 1) * BLK], po[:],
                        recf[:], ALU.mult),
                        wait=[ost_tok.get((idx - 2, slot))], inc="s_fin")
                    fins.append(tfin)
                    ost_tok[(idx, slot)] = em.op(
                        "sync", lambda e, s, slot=slot, g=g, B=B, par=par: e.dma_start(
                            out=oh[g * 256 + slot * P: g * 256 + (slot + 1) * P,
                                   B * BLK:(B + 1) * BLK],
                            in_=ostg[par][:, slot * BLK:(slot + 1) * BLK]),
                        wait=[tfin], inc="s_st", incby=16)
            ofin_tok[idx] = fins + [tdnf]

        emit_pass1_A(0)
        emit_pass1_B(0)
        for idx in range(NT_):
            emit_sweep(idx)
        em.op("sync", lambda e, s: e.nop(),
              wait=[("s_st", em.cnt.get("s_st", 0))])
        em.flush(nc, sems)
    return nc



def build_launch_PROJ(layer):
    Lp = L0 if layer == 0 else L1
    hd = Lp["hd"]
    nic = hd // P
    nc = bass.Bass("TRN2", target_bir_lowering=False)
    xsrc = nc.dram_tensor("xsrc", [TOK, D], F32, kind="ExternalInput").ap()
    ofm = nc.dram_tensor("ofm", [hd, TOK], BF16, kind="ExternalInput").ap()
    gfm = nc.dram_tensor("gfm", [D, TOK], BF16, kind="ExternalInput").ap()
    wpro = nc.dram_tensor("wpro", [hd, D], F32, kind="ExternalInput").ap()
    bprr = nc.dram_tensor("bprr", [P, IC], F32, kind="ExternalInput").ap()
    xout = nc.dram_tensor("xout", [TOK, D], F32, kind="ExternalOutput").ap()
    dfm_dram = nc.dram_tensor("dfm_dram", [D, TOK], BF16, kind="Internal").ap()

    es = ExitStack()
    with es:
        o_sb = es.enter_context(nc.sbuf_tensor("o_sb", [P, nic * TOK], BF16))
        bprr_sb = es.enter_context(nc.sbuf_tensor("bprr_sb", [P, IC], F32))
        wbuf = [es.enter_context(nc.sbuf_tensor(f"wbuf{j}", [P, nic * P], BF16)) for j in range(2)]
        gbuf = [es.enter_context(nc.sbuf_tensor(f"gbuf{j}", [P, TOK], BF16)) for j in range(2)]
        dstg = [es.enter_context(nc.sbuf_tensor(f"dstg{j}", [P, BLK], BF16)) for j in range(2)]
        ps = [es.enter_context(nc.psum_tensor(f"ps{j}", [P, BLK], F32)) for j in range(2)]
        sems = make_sems(nc, es, ["s_ld", "s_w", "s_mm", "s_v", "s_st", "s_kv"])
        em = Em()

        tb = em.op("sync", lambda e, s: e.dma_start(out=bprr_sb[:], in_=bprr),
                   inc="s_ld", incby=16)
        to = None
        for jj in range(4):
            to = em.op("sync", lambda e, s, jj=jj: e.dma_start(
                out=o_sb[:].rearrange("p (a t) -> p a t", t=TOK)[:, jj * (nic // 4):(jj + 1) * (nic // 4), :],
                in_=ofm.rearrange("(a p) t -> p a t", p=P)[:, jj * (nic // 4):(jj + 1) * (nic // 4), :]),
                inc="s_ld", incby=16)

        pm, pe, pst = {}, {}, {}
        for oc in range(IC):
            par = oc % 2
            lw = None
            for jj in range(4):
                lw = em.op("gpsimd", lambda e, s, oc=oc, par=par, jj=jj: e.dma_start(
                    out=wbuf[par][:].rearrange("p (a o) -> p a o", o=P)[:, jj * (nic // 4):(jj + 1) * (nic // 4), :],
                    in_=wpro.rearrange("(a p) o -> p a o", p=P)[:, jj * (nic // 4):(jj + 1) * (nic // 4), oc * P:(oc + 1) * P]),
                    wait=[pm.get((oc - 2, 1))] if jj == 0 else [], inc="s_w", incby=16)
            tg = em.op("gpsimd", lambda e, s, oc=oc, par=par: e.dma_start(
                out=gbuf[par][:], in_=gfm[oc * P:(oc + 1) * P, :]),
                wait=[pe.get((oc - 2, 1))], inc="s_w", incby=16)
            for bi in range(2):
                pp = bi

                def pmm(e, s, par=par, bi=bi, pp=pp):
                    r = None
                    for ic in range(nic):
                        r = e.matmul(ps[pp][:], wbuf[par][:, ic * P:(ic + 1) * P],
                                     o_sb[:, ic * TOK + bi * BLK: ic * TOK + bi * BLK + BLK],
                                     start=(ic == 0), stop=(ic == nic - 1))
                    return r
                tpm = em.op("tensor", pmm, wait=[lw, to, pe.get((oc - 2, bi))], inc="s_mm")
                pm[(oc, bi)] = tpm
                tpe = em.op("vector", lambda e, s, oc=oc, bi=bi, pp=pp, par=par: e.scalar_tensor_tensor(
                    out=dstg[pp][:], in0=ps[pp][:], scalar=bprr_sb[:, oc:oc + 1],
                    in1=gbuf[par][:, bi * BLK:(bi + 1) * BLK],
                    op0=ALU.add, op1=ALU.mult),
                    wait=[tpm, tg, tb, pst.get((oc - 2, bi))], inc="s_v")
                pe[(oc, bi)] = tpe
                pst[(oc, bi)] = em.op("sync", lambda e, s, oc=oc, bi=bi, pp=pp: e.dma_start(
                    out=dfm_dram[oc * P:(oc + 1) * P, bi * BLK:(bi + 1) * BLK],
                    in_=dstg[pp][:]), wait=[tpe], inc="s_st", incby=16)
        dall = em.tok("s_st")

        s3 = ExitStack()
        with s3:
            dtok = [s3.enter_context(nc.sbuf_tensor(f"dtok{j}", [P, D], BF16)) for j in range(2)]
            xtile = [s3.enter_context(nc.sbuf_tensor(f"xtile{j}", [P, D], F32)) for j in range(2)]
            otile = [s3.enter_context(nc.sbuf_tensor(f"otile{j}", [P, D], F32)) for j in range(2)]
            rtk = {}
            for tt in range(NT):
                par = tt % 2
                ttr = em.op("sync", lambda e, s, tt=tt, par=par: e.dma_start_transpose(
                    dtok[par][:], dfm_dram[:, tt * P:(tt + 1) * P]),
                    wait=[dall, rtk.get(tt - 2)], inc="s_kv", incby=16)
                tx = em.op("sync", lambda e, s, tt=tt, par=par: e.dma_start(
                    out=xtile[par][:], in_=xsrc[tt * P:(tt + 1) * P, :]),
                    inc="s_kv", incby=16)
                tadd = em.op("vector", lambda e, s, par=par: e.tensor_tensor(
                    otile[par][:], xtile[par][:], dtok[par][:], ALU.add),
                    wait=[ttr, tx], inc="s_v")
                rtk[tt] = em.op("sync", lambda e, s, tt=tt, par=par: e.dma_start(
                    out=xout[tt * P:(tt + 1) * P, :], in_=otile[par][:]),
                    wait=[tadd], inc="s_st", incby=16)
            em.op("sync", lambda e, s: e.nop(), wait=[("s_st", em.cnt.get("s_st", 0))])
            em.flush(nc, sems)
    return nc



_PROGS = {}


def _get_prog(key, builder):
    if key not in _PROGS:
        _PROGS[key] = builder()
    return _PROGS[key]


def _shard_rows(full):
    return [np.ascontiguousarray(np.concatenate(
        [full[c * BLK:(c + 1) * BLK], full[(15 - c) * BLK:(16 - c) * BLK]], axis=0))
        for c in range(8)]


def _unshard_rows(shards):
    full = np.empty((S,) + shards[0].shape[1:], dtype=shards[0].dtype)
    for c in range(8):
        full[c * BLK:(c + 1) * BLK] = shards[c][0:BLK]
        full[(15 - c) * BLK:(16 - c) * BLK] = shards[c][BLK:2 * BLK]
    return full


def _uncol_shards(mats):
    rows = mats[0].shape[0]
    full = np.empty((rows, S), dtype=mats[0].dtype)
    for c in range(8):
        full[:, c * BLK:(c + 1) * BLK] = mats[c][:, 0:BLK]
        full[:, (15 - c) * BLK:(16 - c) * BLK] = mats[c][:, BLK:2 * BLK]
    return full


def _col_shard(full):
    return [np.ascontiguousarray(np.concatenate(
        [full[:, c * BLK:(c + 1) * BLK], full[:, (15 - c) * BLK:(16 - c) * BLK]], axis=1))
        for c in range(8)]


def kernel(L0_attn_hidden_states, L0_attn_w_attn, L0_attn_b_attn, L0_attn_w_proj,
           L0_attn_b_proj, L0_ffn_weight, L1_attn_w_qkv, L1_attn_b_qkv,
           L1_attn_w_o, L1_attn_b_o, L1_ffn_A, L1_ffn_B, L1_ffn_Wr):
    f32 = np.float32
    x = np.ascontiguousarray(np.asarray(L0_attn_hidden_states, dtype=f32).reshape(S, D))
    x_sh = _shard_rows(x)

    w_attn = np.ascontiguousarray(np.asarray(L0_attn_w_attn, dtype=f32))
    barr0 = np.ascontiguousarray(np.asarray(L0_attn_b_attn, dtype=f32).reshape(96, P).T)
    w_proj = np.ascontiguousarray(np.asarray(L0_attn_w_proj, dtype=f32))
    bprr0 = np.ascontiguousarray(np.asarray(L0_attn_b_proj, dtype=f32).reshape(IC, P).T)
    wffn = np.ascontiguousarray(np.asarray(L0_ffn_weight, dtype=f32))
    wffnT = np.ascontiguousarray(wffn.T)
    wqkv1 = np.ascontiguousarray(np.asarray(L1_attn_w_qkv, dtype=f32).T)
    barr1 = np.ascontiguousarray(np.asarray(L1_attn_b_qkv, dtype=f32).reshape(48, P).T)
    wo1 = np.ascontiguousarray(np.asarray(L1_attn_w_o, dtype=f32).T)
    bprr1 = np.ascontiguousarray(np.asarray(L1_attn_b_o, dtype=f32).reshape(IC, P).T)
    A1 = np.asarray(L1_ffn_A, dtype=f32)
    B1 = np.asarray(L1_ffn_B, dtype=f32)
    Wr1 = np.asarray(L1_ffn_Wr, dtype=f32)
    aT = np.ascontiguousarray(A1.reshape(64, D).T)
    bP = np.ascontiguousarray(B1.transpose(0, 2, 1).reshape(64, D))
    wrT = np.ascontiguousarray(Wr1.T)
    oneh = np.zeros((4, 64), f32)
    for e in range(4):
        oneh[e, e * 16:(e + 1) * 16] = 1.0

    ii = np.arange(P, dtype=f32)
    iotac = np.ascontiguousarray(ii.reshape(P, 1))
    identb = np.ascontiguousarray(np.eye(P, dtype=ml_dtypes.bfloat16))
    hsel = np.zeros((2, 2 * P), f32)
    hsel[0, 0:P] = 1.0
    hsel[1, P:2 * P] = 1.0
    dmask = np.ascontiguousarray((ii[None, :] - ii[:, None]).astype(f32))
    mneg = np.ascontiguousarray(np.where(ii[None, :] > ii[:, None], NEG, 0.0).astype(f32))
    trim = np.ascontiguousarray(
        np.where(ii[:, None] <= ii[None, :], 1.0, 0.0).astype(ml_dtypes.bfloat16))

    cores = list(range(8))

    def run_A(layer, src_shards):
        ncp = _get_prog(("A", layer), lambda: build_launch_A(layer))
        maps = []
        for c in cores:
            m = {"src": src_shards[c]}
            if layer == 0:
                m.update(wqkv=w_attn, barr=barr0, wffn=wffn, wffnT=wffnT)
            else:
                m.update(wqkv=wqkv1, barr=barr1, wrT=wrT, aT=aT, bP=bP, oneh=oneh)
            maps.append(m)
        return run_bass_kernel_spmd(ncp, maps, core_ids=cores).results

    def run_ATT(layer, resA):
        Lp = L0 if layer == 0 else L1
        R = 256 * Lp["G"]
        Qabs = _uncol_shards([r["q_out"] for r in resA])
        Kabs = _uncol_shards([r["k_out"] for r in resA])
        Vsh = [r["v_out"] for r in resA]
        Vabs_sh = _unshard_rows(Vsh)
        ncp = _get_prog(("ATT", layer), lambda: build_launch_ATT(layer))
        maps = []
        kbase = np.arange(64, dtype=f32)[:, None] * P + np.arange(P, dtype=f32)[None, :]
        for c in cores:
            m = dict(
                qh=np.ascontiguousarray(Qabs[c * R:(c + 1) * R, :]),
                kh=np.ascontiguousarray(Kabs[c * R:(c + 1) * R, :]),
                vh=np.ascontiguousarray(Vabs_sh[:, c * R:(c + 1) * R]),
                mneg=mneg, trim=trim, identb=identb, hsel=hsel)
            if layer == 0:
                slp = np.ascontiguousarray(
                    np.tile(SLOPES[4 * c:4 * c + 4][None, :], (P, 1)).astype(f32))
                kvt = np.empty((P, 4 * 64), f32)
                for h in range(4):
                    kvt[:, h * 64:(h + 1) * 64] = (SLOPES[4 * c + h] * kbase).T
                m.update(slp=slp, dmask=dmask, iotac=iotac,
                         kvt=np.ascontiguousarray(kvt))
            maps.append(m)
        res = run_bass_kernel_spmd(ncp, maps, core_ids=cores).results
        Ofull = np.concatenate([r["oh"] for r in res], axis=0)
        return _col_shard(Ofull)

    def run_PROJ(layer, src_shards, o_shards, resA):
        ncp = _get_prog(("PROJ", layer), lambda: build_launch_PROJ(layer))
        maps = [dict(xsrc=src_shards[c], ofm=o_shards[c], gfm=resA[c]["g_out"],
                     wpro=(w_proj if layer == 0 else wo1),
                     bprr=(bprr0 if layer == 0 else bprr1)) for c in cores]
        res = run_bass_kernel_spmd(ncp, maps, core_ids=cores).results
        return [r["xout"] for r in res]

    resA0 = run_A(0, x_sh)
    o0_sh = run_ATT(0, resA0)
    x1_sh = run_PROJ(0, x_sh, o0_sh, resA0)
    resA1 = run_A(1, x1_sh)
    o1_sh = run_ATT(1, resA1)
    out_sh = run_PROJ(1, x1_sh, o1_sh, resA1)

    return _unshard_rows(out_sh).reshape(1, S, D).astype(np.float32)


# version 06 (final)
# speedup vs baseline: 1.4743x; 1.0132x over previous
import numpy as np
import ml_dtypes
from contextlib import ExitStack
import concourse.bass as bass
import concourse.mybir as mybir
from concourse.bass_utils import run_bass_kernel_spmd

F32 = mybir.dt.float32
BF16 = mybir.dt.bfloat16
AF = mybir.ActivationFunctionType
ALU = mybir.AluOpType
AX = mybir.AxisListType

P = 128
TOK = 1024
NT = TOK // P
D = 4096
IC = D // P
S = 8192
BLK = 512
NBLK = S // BLK
NEG = -1.0e30

L0 = dict(H=32, dh=128, hd=4096, oqk=12288, scale=1.0 / 128.0, alibi=True, C=20.0, G=2)
L1 = dict(H=8, dh=256, hd=2048, oqk=6144, scale=1.0, alibi=False, C=90.0, G=1)

ENGS = ("sync", "tensor", "vector", "scalar", "gpsimd")


def alibi_slopes(n):
    import math
    cp2 = 2 ** math.floor(math.log2(n))
    base = 2 ** (-2 ** (-(math.log2(cp2) - 3)))
    sl = base ** np.arange(1, cp2 + 1, dtype=np.float32)
    if cp2 != n:
        eb = 2 ** (-2 ** (-(math.log2(2 * cp2) - 3)))
        n_rem = min(cp2, n - cp2)
        extra = eb ** np.arange(1, 1 + 2 * n_rem, 2, dtype=np.float32)
        sl = np.concatenate([sl, extra])
    return sl.astype(np.float32)


SLOPES = alibi_slopes(32)


class Em:

    def __init__(self):
        self.ops = []
        self.cnt = {}
        self.lastwait = {e: {} for e in ENGS}

    def tok(self, sem):
        return (sem, self.cnt.get(sem, 0))

    AUTO = {"vector": "s_v", "scalar": "s_a"}

    def op(self, eng, fn, wait=(), inc=None, incby=1):
        if eng in self.AUTO:
            auto = self.AUTO[eng]
            wait = list(wait) + [(auto, self.cnt.get(auto, 0))]
            inc, incby = auto, 1
        waits = []
        for wt in wait:
            if wt is None:
                continue
            sem, c = wt
            if c <= 0 or self.lastwait[eng].get(sem, -1) >= c:
                continue
            self.lastwait[eng][sem] = c
            waits.append((sem, c))
        ret = None
        if inc is not None:
            self.cnt[inc] = self.cnt.get(inc, 0) + incby
            ret = (inc, self.cnt[inc])

        def run(engine, sems, fn=fn, waits=waits, inc=inc, incby=incby):
            for s0, c0 in waits:
                engine.wait_ge(sems[s0], c0)
            r = fn(engine, sems)
            if inc is not None:
                r.then_inc(sems[inc], incby)

        self.ops.append((eng, run))
        return ret

    def flush(self, nc, sems):
        if not self.ops:
            return
        bes = ExitStack()
        with bes:
            block = bes.enter_context(nc.Block())
            for name in ENGS:
                ops = [fn for e, fn in self.ops if e == name]
                if not ops:
                    continue
                reg = {"sync": block.sync, "tensor": block.tensor,
                       "vector": block.vector, "scalar": block.scalar,
                       "gpsimd": block.gpsimd}[name]

                def body(engine, ops=ops):
                    for fn in ops:
                        fn(engine, sems)
                reg(body)
        self.ops = []


def make_sems(nc, es, names):
    return {n: es.enter_context(nc.semaphore(n)) for n in names}



def emit_ln(em, src, xn_dram, xn_fm, bufs):
    xt, xc, sqs, xnt, mu, vs, rstd, eps = bufs
    teps = em.op("gpsimd", lambda e, s: e.memset(eps[:, 0:1], 1e-5), inc="s_g")
    last_store = None
    for t in range(NT):
        tl = em.op("sync", lambda e, s, t=t: e.dma_start(
            out=xt[:], in_=src[t * P:(t + 1) * P, :]),
            wait=[last_store], inc="s_ld", incby=16)
        em.op("vector", lambda e, s: e.reduce_sum(mu[:, 0:1], xt[:], AX.X), wait=[tl])
        em.op("vector", lambda e, s: e.tensor_scalar_mul(mu[:, 0:1], mu[:, 0:1], 1.0 / D))
        txc = em.op("vector", lambda e, s: e.tensor_scalar(
            xc[:], xt[:], mu[:, 0:1], None, ALU.subtract), inc="s_v")
        tsq = em.op("scalar", lambda e, s: e.activation(
            sqs[:], xc[:], AF.Square), wait=[txc], inc="s_a")
        tv = em.op("vector", lambda e, s: e.reduce_sum(
            vs[:, 0:1], sqs[:], AX.X), wait=[tsq], inc="s_v")
        em.op("scalar", lambda e, s: e.activation(
            rstd[:, 0:1], vs[:, 0:1], AF.Ln, bias=eps[:, 0:1], scale=1.0 / D),
            wait=[tv, teps])
        ta = em.op("scalar", lambda e, s: e.activation(
            rstd[:, 0:1], rstd[:, 0:1], AF.Exp, bias=0.0, scale=-0.5), inc="s_a")
        tx = em.op("vector", lambda e, s: e.tensor_scalar(
            xnt[:], xc[:], rstd[:, 0:1], None, ALU.mult), wait=[ta], inc="s_v")
        last_store = em.op("sync", lambda e, s, t=t: e.dma_start(
            out=xn_dram[t * P:(t + 1) * P, :], in_=xnt[:]), wait=[tx], inc="s_st", incby=16)
    fm_tok = None
    if xn_fm is not None:
        for ic in range(IC):
            fm_tok = em.op("sync", lambda e, s, ic=ic: e.dma_start_transpose(
                xn_fm[:, ic * TOK:(ic + 1) * TOK], xn_dram[:, ic * P:(ic + 1) * P]),
                wait=[last_store], inc="s_fm", incby=16)
    return last_store, fm_tok


def emit_qkv(em, Lp, wqkv_dram, barr_sb, barr_tok, xn_fm, fm_tok, wbuf, stg, ps,
             q_out, k_out, vfm_dram):
    hd, oqk = Lp["hd"], Lp["oqk"]
    noc, nQ = oqk // P, hd // P
    mm_toks, ev_toks, st_toks = {}, {}, {}
    for oc in range(noc):
        par = oc % 2
        lastw = None
        for jj in range(4):
            lastw = em.op("gpsimd", lambda e, s, oc=oc, par=par, jj=jj: e.dma_start(
                out=wbuf[par][:].rearrange("p (a o) -> p a o", o=P)[:, jj * 8:(jj + 1) * 8, :],
                in_=wqkv_dram.rearrange("(a p) o -> p a o", p=P)[:, jj * 8:(jj + 1) * 8, oc * P:(oc + 1) * P]),
                wait=[mm_toks.get((oc - 2, 1))] if jj == 0 else [], inc="s_w", incby=16)
        for th in range(2):
            step = 2 * oc + th
            pp = th

            def mms(e, s, par=par, pp=pp, th=th):
                r = None
                for ic in range(IC):
                    r = e.matmul(ps[pp][:],
                                 wbuf[par][:, ic * P:(ic + 1) * P],
                                 xn_fm[:, ic * TOK + th * BLK: ic * TOK + th * BLK + BLK],
                                 start=(ic == 0), stop=(ic == IC - 1))
                return r
            tmm = em.op("tensor", mms, wait=[lastw, fm_tok, ev_toks.get(step - 2)], inc="s_mm")
            mm_toks[(oc, th)] = tmm
            tev = em.op("scalar", lambda e, s, pp=pp, oc=oc: e.activation(
                stg[pp][:], ps[pp][:], AF.Identity, bias=barr_sb[:, oc:oc + 1], scale=1.0),
                wait=[tmm, st_toks.get(step - 2), barr_tok], inc="s_ev")
            ev_toks[step] = tev
            if oc < nQ:
                dst, row = q_out, oc * P
            elif oc < 2 * nQ:
                dst, row = k_out, (oc - nQ) * P
            else:
                dst, row = vfm_dram, (oc - 2 * nQ) * P
            st_toks[step] = em.op("sync", lambda e, s, dst=dst, row=row, th=th, pp=pp: e.dma_start(
                out=dst[row:row + P, th * BLK:(th + 1) * BLK], in_=stg[pp][:]),
                wait=[tev], inc="s_st", incby=16)
    return st_toks[2 * noc - 1], ev_toks[2 * noc - 1], ev_toks[2 * noc - 2]


def emit_v_transpose(em, Lp, vfm_dram, v_out, vstg, vdone_tok):
    last = {}
    for tt in range(NT):
        par = tt % 2
        ttr = em.op("sync", lambda e, s, par=par, tt=tt: e.dma_start_transpose(
            vstg[par][:], vfm_dram[:, tt * P:(tt + 1) * P]),
            wait=[vdone_tok, last.get(tt - 2)], inc="s_vt", incby=16)
        last[tt] = em.op("sync", lambda e, s, par=par, tt=tt: e.dma_start(
            out=v_out[tt * P:(tt + 1) * P, :], in_=vstg[par][:]),
            wait=[ttr], inc="s_vt", incby=16)
    return last[NT - 1]


def emit_moe(nc, em, scope, xn_fm, fm_tok, wrT, aT, bP, oneh, g_out, ps, psum_free):
    wr_sb = scope.enter_context(nc.sbuf_tensor("wr_sb", [P, IC * 4], BF16))
    aT_sb = scope.enter_context(nc.sbuf_tensor("aT_sb", [P, IC * 64], BF16))
    bP_sb = scope.enter_context(nc.sbuf_tensor("bP_sb", [64, D], BF16))
    oneh_sb = scope.enter_context(nc.sbuf_tensor("oneh_sb", [4, 64], F32))
    ident = scope.enter_context(nc.sbuf_tensor("identM", [P, P], F32))
    lg = scope.enter_context(nc.sbuf_tensor("lg", [P, 4], F32))
    pmin = scope.enter_context(nc.sbuf_tensor("pmin", [P, 8], F32))
    nm1 = scope.enter_context(nc.sbuf_tensor("nm1", [P, 1], F32))
    t2s = scope.enter_context(nc.sbuf_tensor("t2s", [P, 1], F32))
    ex4 = scope.enter_context(nc.sbuf_tensor("ex4", [P, 4], F32))
    ind4 = scope.enter_context(nc.sbuf_tensor("ind4", [P, 4], F32))
    den = scope.enter_context(nc.sbuf_tensor("den", [P, 1], F32))
    rec = scope.enter_context(nc.sbuf_tensor("rec", [P, 1], F32))
    g4 = scope.enter_context(nc.sbuf_tensor("g4", [P, 4], F32))
    gates_fm = scope.enter_context(nc.sbuf_tensor("gates_fm", [4, TOK], F32))
    gbc_sb = scope.enter_context(nc.sbuf_tensor("gbc_sb", [64, BLK], F32))
    smid = scope.enter_context(nc.sbuf_tensor("smid", [64, BLK], BF16))
    gstg = [scope.enter_context(nc.sbuf_tensor(f"gstg{j}", [P, BLK], BF16)) for j in range(2)]
    ps_l = scope.enter_context(nc.psum_tensor("ps_l", [P, BLK], F32))
    ps_t = scope.enter_context(nc.psum_tensor("ps_t", [P, BLK], F32))
    ps_m = ps[0][0:64, :]
    ps_g = ps[1][0:64, :]

    t1 = em.op("gpsimd", lambda e, s: e.dma_start(
        out=wr_sb[:].rearrange("p (a r) -> p a r", r=4),
        in_=wrT.rearrange("(a p) r -> p a r", p=P)), inc="s_w", incby=16)
    t2_ = em.op("gpsimd", lambda e, s: e.dma_start(
        out=aT_sb[:].rearrange("p (a r) -> p a r", r=64),
        in_=aT.rearrange("(a p) r -> p a r", p=P)), inc="s_w", incby=16)
    t3 = em.op("gpsimd", lambda e, s: e.dma_start(out=bP_sb[:], in_=bP), inc="s_w", incby=16)
    t4 = em.op("sync", lambda e, s: e.dma_start(out=oneh_sb[:], in_=oneh), inc="s_ld", incby=16)
    tms = em.op("gpsimd", lambda e, s: e.memset(ident[:], 0.0), inc="s_g")
    ti = em.op("gpsimd", lambda e, s: e.affine_select(
        out=ident[:], in_=ident[:], compare_op=ALU.not_equal, fill=1.0,
        base=0, pattern=[[-1, P]], channel_multiplier=1), wait=[tms], inc="s_g")

    pairs = [(0, 1), (0, 2), (0, 3), (1, 2), (1, 3), (2, 3)]
    gtok = {}
    prev_tr = None
    for t in range(NT):
        def lmm(e, s, t=t):
            r = None
            for ic in range(IC):
                r = e.matmul(ps_l[:, 0:4],
                             xn_fm[:, ic * TOK + t * P: ic * TOK + (t + 1) * P],
                             wr_sb[:, ic * 4:(ic + 1) * 4],
                             start=(ic == 0), stop=(ic == IC - 1))
            return r
        tml = em.op("tensor", lmm, wait=[t1, fm_tok, gtok.get(t - 1)], inc="s_mm")
        em.op("vector", lambda e, s: e.tensor_copy(lg[:], ps_l[:, 0:4]), wait=[tml])
        for i, (a, b) in enumerate(pairs):
            em.op("vector", lambda e, s, i=i, a=a, b=b: e.tensor_tensor(
                pmin[:, i:i + 1], lg[:, a:a + 1], lg[:, b:b + 1], ALU.min))
        em.op("vector", lambda e, s: e.tensor_tensor(pmin[:, 6:7], pmin[:, 0:1], pmin[:, 1:2], ALU.max))
        for i in (2, 3, 4):
            em.op("vector", lambda e, s, i=i: e.tensor_tensor(
                pmin[:, 6:7], pmin[:, 6:7], pmin[:, i:i + 1], ALU.max))
        em.op("vector", lambda e, s: e.tensor_tensor(t2s[:, 0:1], pmin[:, 6:7], pmin[:, 5:6], ALU.max))
        em.op("vector", lambda e, s: e.tensor_reduce(nm1[:, 0:1], lg[:], AX.X, ALU.max, negate=True))
        tind = em.op("vector", lambda e, s: e.tensor_scalar(
            ind4[:], lg[:], t2s[:, 0:1], None, ALU.is_ge), inc="s_v")
        tex = em.op("scalar", lambda e, s: e.activation(
            ex4[:], lg[:], AF.Exp, bias=nm1[:, 0:1], scale=1.0), wait=[tind], inc="s_a")
        em.op("vector", lambda e, s: e.tensor_tensor(g4[:], ind4[:], ex4[:], ALU.mult), wait=[tex])
        em.op("vector", lambda e, s: e.reduce_sum(den[:, 0:1], g4[:], AX.X))
        em.op("vector", lambda e, s: e.reciprocal(rec[:, 0:1], den[:, 0:1]))
        tg4 = em.op("vector", lambda e, s: e.tensor_scalar(
            g4[:], g4[:], rec[:, 0:1], None, ALU.mult), inc="s_v")
        ttr = em.op("tensor", lambda e, s: e.transpose(ps_t[0:4, 0:P], g4[:], ident[:]),
                    wait=[tg4, ti, prev_tr], inc="s_mm")
        prev_tr = gtok[t] = em.op("vector", lambda e, s, t=t: e.tensor_copy(
            gates_fm[:, t * P:(t + 1) * P], ps_t[0:4, 0:P]), wait=[ttr], inc="s_v")

    dguard = {}
    last = None
    for th in range(2):
        def midmm(e, s, th=th):
            r = None
            for ic in range(IC):
                r = e.matmul(ps_m, aT_sb[:, ic * 64:(ic + 1) * 64],
                             xn_fm[:, ic * TOK + th * BLK: ic * TOK + th * BLK + BLK],
                             start=(ic == 0), stop=(ic == IC - 1))
            return r
        tmid = em.op("tensor", midmm, wait=[t2_, gtok[NT - 1]] + psum_free, inc="s_mm")
        tgb = em.op("tensor", lambda e, s, th=th: e.matmul(
            ps_g, oneh_sb[:], gates_fm[:, th * BLK:(th + 1) * BLK],
            start=True, stop=True), wait=[t4], inc="s_mm")
        em.op("vector", lambda e, s: e.tensor_copy(gbc_sb[:], ps_g),
              wait=[tgb, dguard.get(("sm", th - 1))])
        tsm = em.op("vector", lambda e, s: e.scalar_tensor_tensor(
            out=smid[:], in0=ps_m, scalar=4.0, in1=gbc_sb[:],
            op0=ALU.mult, op1=ALU.mult), wait=[tmid], inc="s_v")
        dguard[("sm", th)] = tsm
        for oc in range(IC):
            pp = oc % 2
            pdst = ps_l if pp == 0 else ps_t
            tdm = em.op("tensor", lambda e, s, oc=oc, pdst=pdst: e.matmul(
                pdst[:], bP_sb[:, oc * P:(oc + 1) * P], smid[:],
                start=True, stop=True),
                wait=[tsm, t3, dguard.get((th, oc - 2))], inc="s_mm")
            tev = em.op("scalar", lambda e, s, pdst=pdst, pp=pp: e.activation(
                gstg[pp][:], pdst[:], AF.Silu),
                wait=[tdm, dguard.get(("st", th, oc - 2))], inc="s_a")
            dguard[(th, oc)] = tev
            last = em.op("sync", lambda e, s, oc=oc, th=th, pp=pp: e.dma_start(
                out=g_out[oc * P:(oc + 1) * P, th * BLK:(th + 1) * BLK],
                in_=gstg[pp][:]), wait=[tev], inc="s_st", incby=16)
            dguard[("st", th, oc)] = last
    return last


def build_launch_A(layer, debug=False):
    Lp = L0 if layer == 0 else L1
    nc = bass.Bass("TRN2", target_bir_lowering=False)
    src = nc.dram_tensor("src", [TOK, D], F32, kind="ExternalInput").ap()
    hd, oqk = Lp["hd"], Lp["oqk"]
    noc = oqk // P
    wqkv = nc.dram_tensor("wqkv", [D, oqk], F32, kind="ExternalInput").ap()
    barr = nc.dram_tensor("barr", [P, noc], F32, kind="ExternalInput").ap()
    if layer == 0:
        wffn = nc.dram_tensor("wffn", [D, D], F32, kind="ExternalInput").ap()
        wffnT = nc.dram_tensor("wffnT", [D, D], F32, kind="ExternalInput").ap()
    else:
        wrT = nc.dram_tensor("wrT", [D, 4], F32, kind="ExternalInput").ap()
        aT = nc.dram_tensor("aT", [D, 64], F32, kind="ExternalInput").ap()
        bP = nc.dram_tensor("bP", [64, D], F32, kind="ExternalInput").ap()
        oneh = nc.dram_tensor("oneh", [4, 64], F32, kind="ExternalInput").ap()

    q_out = nc.dram_tensor("q_out", [hd, TOK], BF16, kind="ExternalOutput").ap()
    k_out = nc.dram_tensor("k_out", [hd, TOK], BF16, kind="ExternalOutput").ap()
    v_out = nc.dram_tensor("v_out", [TOK, hd], BF16, kind="ExternalOutput").ap()
    g_out = nc.dram_tensor("g_out", [D, TOK], BF16, kind="ExternalOutput").ap()
    xn_dram = nc.dram_tensor("xn_dram", [TOK, D], BF16, kind="ExternalOutput" if debug else "Internal").ap()
    vfm_dram = nc.dram_tensor("vfm_dram", [hd, TOK], BF16, kind="Internal").ap()

    es = ExitStack()
    with es:
        wbuf = [es.enter_context(nc.sbuf_tensor(f"wbuf{j}", [P, IC * P], BF16)) for j in range(2)]
        stg = [es.enter_context(nc.sbuf_tensor(f"stg{j}", [P, BLK], BF16)) for j in range(2)]
        vstg = [es.enter_context(nc.sbuf_tensor(f"vstg{j}", [P, hd], BF16)) for j in range(2)]
        barr_sb = es.enter_context(nc.sbuf_tensor("barr_sb", [P, noc], F32))
        ps = [es.enter_context(nc.psum_tensor(f"ps{j}", [P, BLK], F32)) for j in range(2)]
        sems = make_sems(nc, es, ["s_ld", "s_v", "s_a", "s_st", "s_fm", "s_w",
                                  "s_mm", "s_ev", "s_vt", "s_g"])
        em = Em()
        barr_tok = em.op("sync", lambda e, s: e.dma_start(out=barr_sb[:], in_=barr),
                         inc="s_ld", incby=16)

        s1 = ExitStack()
        with s1:
            xn_fm = s1.enter_context(nc.sbuf_tensor("xn_fm", [P, IC * TOK], BF16))
            xt = s1.enter_context(nc.sbuf_tensor("xt", [P, D], F32))
            xc = s1.enter_context(nc.sbuf_tensor("xc", [P, D], F32))
            sqs = s1.enter_context(nc.sbuf_tensor("sqs", [P, D], F32))
            xnt = s1.enter_context(nc.sbuf_tensor("xnt", [P, D], BF16))
            mu = s1.enter_context(nc.sbuf_tensor("mu", [P, 1], F32))
            vs = s1.enter_context(nc.sbuf_tensor("vs", [P, 1], F32))
            rstd = s1.enter_context(nc.sbuf_tensor("rstd", [P, 1], F32))
            epsb = s1.enter_context(nc.sbuf_tensor("epsb", [P, 1], F32))

            _, fm_tok = emit_ln(em, src, xn_dram, xn_fm, (xt, xc, sqs, xnt, mu, vs, rstd, epsb))
            vlast, evl, evl2 = emit_qkv(em, Lp, wqkv, barr_sb, barr_tok, xn_fm, fm_tok,
                                        wbuf, stg, ps, q_out, k_out, vfm_dram)
            emit_v_transpose(em, Lp, vfm_dram, v_out, vstg, vlast)
            if layer == 1:
                emit_moe(nc, em, s1, xn_fm, fm_tok, wrT, aT, bP, oneh, g_out, ps,
                         [evl, evl2])
            em.flush(nc, sems)

        if layer == 0:
            s2 = ExitStack()
            with s2:
                sx_fm = s2.enter_context(nc.sbuf_tensor("sx_fm", [P, IC * TOK], BF16))
                fmchunk = s2.enter_context(nc.sbuf_tensor("fmchunk", [P, TOK], BF16))
                wnat = s2.enter_context(nc.sbuf_tensor("wnat", [P, D], BF16))
                snat = s2.enter_context(nc.sbuf_tensor("snat", [P, D], BF16))
                absout = s2.enter_context(nc.sbuf_tensor("absout", [P, D], BF16))
                swTr = s2.enter_context(nc.sbuf_tensor("swTr", [P, IC * P], BF16))
                swTs = [s2.enter_context(nc.sbuf_tensor(f"swTs{j}", [P, IC * P], BF16)) for j in range(2)]
                scale_arr = s2.enter_context(nc.sbuf_tensor("scale_arr", [P, IC], F32))
                asum = s2.enter_context(nc.sbuf_tensor("asum", [P, 1], F32))

                sg_tok = None
                for ic in range(IC):
                    tt = em.op("sync", lambda e, s, ic=ic: e.dma_start_transpose(
                        fmchunk[:], xn_dram[:, ic * P:(ic + 1) * P]),
                        wait=[sg_tok], inc="s_fm", incby=16)
                    sg_tok = em.op("scalar", lambda e, s, ic=ic: e.activation(
                        sx_fm[:, ic * TOK:(ic + 1) * TOK], fmchunk[:], AF.Sign),
                        wait=[tt], inc="s_a")

                mmk, evk, stk = {}, {}, {}
                tabs_prev, tsg_prev = None, None
                for oc in range(IC):
                    par = oc % 2
                    tn = em.op("gpsimd", lambda e, s, oc=oc: e.dma_start(
                        out=wnat[:], in_=wffn[oc * P:(oc + 1) * P, :]),
                        wait=[tabs_prev], inc="s_w", incby=16)
                    tsn = em.op("scalar", lambda e, s: e.activation(
                        absout[:], wnat[:], AF.Abs), wait=[tn], inc="s_a")
                    tabs_prev = em.op("vector", lambda e, s: e.reduce_sum(
                        asum[:, 0:1], absout[:], AX.X), wait=[tsn], inc="s_v")
                    tsc = em.op("vector", lambda e, s, oc=oc: e.tensor_scalar_mul(
                        scale_arr[:, oc:oc + 1], asum[:, 0:1], 1.0 / D), inc="s_g")
                    lw = None
                    for jj in range(4):
                        lw = em.op("gpsimd", lambda e, s, oc=oc, jj=jj: e.dma_start(
                            out=swTr[:].rearrange("p (a o) -> p a o", o=P)[:, jj * 8:(jj + 1) * 8, :],
                            in_=wffnT.rearrange("(a p) o -> p a o", p=P)[:, jj * 8:(jj + 1) * 8, oc * P:(oc + 1) * P]),
                            wait=[tsg_prev] if jj == 0 else [], inc="s_w", incby=16)
                    tsg_prev = em.op("scalar", lambda e, s, par=par: e.activation(
                        swTs[par][:], swTr[:], AF.Sign),
                        wait=[lw, mmk.get((oc - 2, 1))], inc="s_a")
                    for th in range(2):
                        step = 2 * oc + th
                        pp = th

                        def mms(e, s, par=par, pp=pp, th=th):
                            r = None
                            for ic in range(IC):
                                r = e.matmul(ps[pp][:],
                                             swTs[par][:, ic * P:(ic + 1) * P],
                                             sx_fm[:, ic * TOK + th * BLK: ic * TOK + th * BLK + BLK],
                                             start=(ic == 0), stop=(ic == IC - 1))
                            return r
                        tmm = em.op("tensor", mms,
                                    wait=[tsg_prev, sg_tok, evk.get(step - 2)], inc="s_mm")
                        mmk[(oc, th)] = tmm
                        tev = em.op("scalar", lambda e, s, pp=pp, oc=oc: e.activation(
                            stg[pp][:], ps[pp][:], AF.Silu,
                            bias=0.0, scale=scale_arr[:, oc:oc + 1]),
                            wait=[tmm, tsc, stk.get(step - 2)], inc="s_ev")
                        evk[step] = tev
                        stk[step] = em.op("sync", lambda e, s, oc=oc, th=th, pp=pp: e.dma_start(
                            out=g_out[oc * P:(oc + 1) * P, th * BLK:(th + 1) * BLK],
                            in_=stg[pp][:]), wait=[tev], inc="s_st", incby=16)
                em.flush(nc, sems)

        em.op("sync", lambda e, s: e.nop(),
              wait=[("s_st", em.cnt.get("s_st", 0)), ("s_vt", em.cnt.get("s_vt", 0))])
        em.flush(nc, sems)
    return nc




_ATT_DBG = 0


def build_launch_ATT(layer):
    DBG = _ATT_DBG
    Lp = L0 if layer == 0 else L1
    G = Lp["G"]
    C, sc = Lp["C"], Lp["scale"]
    R = 256 * G
    nheads = 2 if layer == 0 else 1
    NCH = R // P
    DEPTH = 3
    NKC = S // P

    nc = bass.Bass("TRN2", target_bir_lowering=False)
    qh = nc.dram_tensor("qh", [R, S], BF16, kind="ExternalInput").ap()
    kh = nc.dram_tensor("kh", [R, S], BF16, kind="ExternalInput").ap()
    vh = nc.dram_tensor("vh", [S, R], BF16, kind="ExternalInput").ap()
    mneg = nc.dram_tensor("mneg", [P, P], F32, kind="ExternalInput").ap()
    trim = nc.dram_tensor("trim", [P, P], BF16, kind="ExternalInput").ap()
    identd = nc.dram_tensor("identb", [P, P], BF16, kind="ExternalInput").ap()
    hseld = nc.dram_tensor("hsel", [2, 2 * P], F32, kind="ExternalInput").ap()
    if layer == 0:
        slp = nc.dram_tensor("slp", [P, 4], F32, kind="ExternalInput").ap()
        dmask = nc.dram_tensor("dmask", [P, P], F32, kind="ExternalInput").ap()
        iotac = nc.dram_tensor("iotac", [P, 1], F32, kind="ExternalInput").ap()
        kvt = nc.dram_tensor("kvt", [P, 4 * NKC], F32, kind="ExternalInput").ap()
    oh = nc.dram_tensor("oh", [R, S], BF16, kind="ExternalOutput").ap()

    es = ExitStack()
    with es:
        kres = es.enter_context(nc.sbuf_tensor("kres", [P, NCH * S], BF16))
        vres = es.enter_context(nc.sbuf_tensor("vres", [P, NKC * R], BF16))
        qsb = [es.enter_context(nc.sbuf_tensor(f"qsb{j}", [P, 2 * BLK], BF16))
               for j in range(2)]
        pt = [es.enter_context(nc.sbuf_tensor(f"pt{j}", [P, BLK], BF16))
              for j in range(DEPTH)]
        bias_bc = [[es.enter_context(nc.sbuf_tensor(f"bbc{h}_{j}", [P, BLK], BF16))
                    for j in range(2)] for h in range(nheads)]
        biasrow = es.enter_context(nc.sbuf_tensor("biasrow", [1, BLK], BF16))
        trim_sb = es.enter_context(nc.sbuf_tensor("trim_sb", [P, P], BF16))
        identb = es.enter_context(nc.sbuf_tensor("identb_sb", [P, P], BF16))
        mneg_sb = es.enter_context(nc.sbuf_tensor("mneg_sb", [P, P], F32))
        nth = 4 if layer == 0 else 1
        tile_h = [es.enter_context(nc.sbuf_tensor(f"tile_h{h}", [P, P], F32))
                  for h in range(nth)]
        sdiag = es.enter_context(nc.sbuf_tensor("sdiag", [P, P], F32))
        m0c = es.enter_context(nc.sbuf_tensor("m0c", [P, 1], F32))
        altc = es.enter_context(nc.sbuf_tensor("altc", [P, 1], F32))
        t1c = es.enter_context(nc.sbuf_tensor("t1c", [P, 1], F32))
        bcol = es.enter_context(nc.sbuf_tensor("bcol", [P, 8], BF16))
        km = es.enter_context(nc.sbuf_tensor("km", [P, 2], F32))
        kmb = es.enter_context(nc.sbuf_tensor("kmb", [P, 16], BF16))
        denf = es.enter_context(nc.sbuf_tensor("denf", [2, BLK], F32))
        recf = es.enter_context(nc.sbuf_tensor("recf", [P, BLK], F32))
        ostg = [es.enter_context(nc.sbuf_tensor(f"ostg{j}", [P, 2 * BLK], BF16))
                for j in range(2)]
        oneh = [es.enter_context(nc.sbuf_tensor(f"oneh{h}", [P, 2], BF16))
                for h in range(nheads)]
        hsel_sb = es.enter_context(nc.sbuf_tensor("hsel_sb", [2, 2 * P], F32))
        onesr_b = es.enter_context(nc.sbuf_tensor("onesr_b", [1, P], BF16))
        if layer == 0:
            slp_sb = es.enter_context(nc.sbuf_tensor("slp_sb", [P, 4], F32))
            dm_sb = es.enter_context(nc.sbuf_tensor("dm_sb", [P, P], F32))
            iot_sb = es.enter_context(nc.sbuf_tensor("iot_sb", [P, 1], F32))
            kvt_sb = es.enter_context(nc.sbuf_tensor("kvt_sb", [P, 4 * NKC], F32))
            r60 = es.enter_context(nc.sbuf_tensor("r60", [P, 4], F32))

        ps_s = [es.enter_context(nc.psum_tensor(f"ps_s{j}", [P, BLK], F32))
                for j in range(DEPTH)]
        ps_o = [es.enter_context(nc.psum_tensor(f"ps_o{j}", [P, BLK], F32))
                for j in range(2)]
        ps_d = es.enter_context(nc.psum_tensor("ps_d", [2, BLK], F32))
        ps_p1 = es.enter_context(nc.psum_tensor("ps_p1", [P, P], F32))
        ps_p1b = es.enter_context(nc.psum_tensor("ps_p1b", [P, BLK], F32))

        sems = make_sems(nc, es, ["s_ld", "s_w", "s_q", "s_mm", "s_a", "s_v",
                                  "s_st", "s_fin", "s_g"])
        em = Em()

        consts = [(trim_sb, trim), (identb, identd), (mneg_sb, mneg),
                  (hsel_sb, hseld)]
        if layer == 0:
            consts += [(slp_sb, slp), (dm_sb, dmask), (iot_sb, iotac),
                       (kvt_sb, kvt)]
        for buf, dram in consts:
            em.op("sync", lambda e, s, buf=buf, dram=dram: e.dma_start(
                out=buf[:], in_=dram), inc="s_ld", incby=16)
        cst = em.tok("s_ld")
        em.op("gpsimd", lambda e, s: e.memset(kmb[:], 0.0))
        for h in range(nheads):
            em.op("gpsimd", lambda e, s, h=h: e.memset(oneh[h][:], 0.0))
            em.op("gpsimd", lambda e, s, h=h: e.memset(oneh[h][:, h:h + 1], 1.0))
        gcst = em.op("gpsimd", lambda e, s: e.memset(onesr_b[:], 1.0), inc="s_g")
        for c in range(NCH):
            em.op("gpsimd", lambda e, s, c=c: e.dma_start(
                out=kres[:, c * S:(c + 1) * S], in_=kh[c * P:(c + 1) * P, :]),
                inc="s_w", incby=16)
        for t0 in range(0, NKC, 8):
            em.op("gpsimd", lambda e, s, t0=t0: e.dma_start(
                out=vres[:].rearrange("p (t r) -> p t r", r=R)[:, t0:t0 + 8, :],
                in_=vh.rearrange("(t p) r -> p t r", p=P)[:, t0:t0 + 8, :]),
                inc="s_w", incby=16)
        kv_tok = em.tok("s_w")
        if layer == 0:
            for h in range(4):
                em.op("vector", lambda e, s, h=h: e.tensor_scalar(
                    tile_h[h][:], dm_sb[:], slp_sb[:, h:h + 1], None, ALU.mult),
                    wait=[cst])
                em.op("vector", lambda e, s, h=h: e.tensor_tensor(
                    tile_h[h][:], tile_h[h][:], mneg_sb[:], ALU.add))
            em.op("vector", lambda e, s: e.tensor_scalar(
                r60[:], slp_sb[:], 127.0, -60.0, ALU.mult, ALU.add))
        else:
            em.op("vector", lambda e, s: e.tensor_copy(tile_h[0][:], mneg_sb[:]),
                  wait=[cst])
        th_tok = em.op("vector", lambda e, s: e.nop(), inc="s_v")

        tasks = [(g, B) for g in range(G) for B in range(NBLK)]
        if DBG & 8:
            tasks = tasks[:1]
        NT_ = len(tasks)
        tq_tok = {}
        bias_tok = {}
        qsb_free = {}
        bbc_free = {}
        p1_free = [None]
        p1b_free = [None]
        ofin_tok = {}
        ost_tok = {}
        exp_toks = {}
        pv_toks = {}
        gstep = [0]

        def emit_pass1_A(idx):
            g, B = tasks[idx]
            par = idx % 2
            q0 = B * BLK
            tq_tok[idx] = em.op("sync", lambda e, s, g=g, B=B, par=par: e.dma_start(
                out=qsb[par][:].rearrange("p (a t) -> p a t", a=2),
                in_=qh[g * 256:(g + 1) * 256, B * BLK:(B + 1) * BLK]
                .rearrange("(a p) t -> p a t", p=P)),
                wait=[qsb_free.get(idx)], inc="s_q", incby=16)
            if DBG & 1:
                for hh in range(nheads):
                    bias_tok[(idx, hh)] = em.op(
                        "vector", lambda e, s, hh=hh, par=par: e.memset(
                            bias_bc[hh][par][:], 0.0),
                        wait=[bbc_free.get((hh, idx))], inc="s_v")
                return
            for j in range(2 if layer == 1 else nheads):
                c = (g * 2 + j) if layer == 0 else j
                em.op("vector", lambda e, s, j=j, c=c, q0=q0: e.reduce_sum(
                    km[:, j:j + 1], kres[:, c * S + q0: c * S + q0 + BLK], AX.X),
                    wait=[kv_tok, p1_free[0]])
                em.op("vector", lambda e, s, j=j: e.tensor_scalar_mul(
                    km[:, j:j + 1], km[:, j:j + 1], 1.0 / BLK))
            tkmb = None
            for j in range(2 if layer == 1 else nheads):
                tkmb = em.op("vector", lambda e, s, j=j: e.tensor_copy(
                    kmb[:, j * 8:j * 8 + 1], km[:, j:j + 1]), inc="s_v")
            prev_stt = None
            for hh in range(nheads):
                hsl = g * 2 + hh
                th = tile_h[hsl] if layer == 0 else tile_h[0]
                for sdx in range(4):
                    ktc = q0 + sdx * P

                    def dsmm(e, s, hh=hh, sdx=sdx, ktc=ktc, g=g, par=par):
                        if layer == 0:
                            c = g * 2 + hh
                            return e.matmul(
                                ps_p1[:, 0:P],
                                qsb[par][:, hh * BLK + sdx * P: hh * BLK + (sdx + 1) * P],
                                kres[:, c * S + ktc: c * S + ktc + P],
                                start=True, stop=True)
                        e.matmul(ps_p1[:, 0:P],
                                 qsb[par][:, sdx * P:(sdx + 1) * P],
                                 kres[:, ktc: ktc + P], start=True, stop=False)
                        return e.matmul(
                            ps_p1[:, 0:P],
                            qsb[par][:, BLK + sdx * P: BLK + (sdx + 1) * P],
                            kres[:, S + ktc: S + ktc + P], start=False, stop=True)
                    tdq = em.op("tensor", dsmm,
                                wait=[tq_tok[idx], kv_tok, prev_stt, p1_free[0]],
                                inc="s_mm")

                    def altm(e, s, hh=hh, sdx=sdx, par=par):
                        if layer == 0:
                            return e.matmul(
                                ps_p1b[:, 0:8],
                                qsb[par][:, hh * BLK + sdx * P: hh * BLK + (sdx + 1) * P],
                                kmb[:, hh * 8:(hh + 1) * 8], start=True, stop=True)
                        e.matmul(ps_p1b[:, 0:8],
                                 qsb[par][:, sdx * P:(sdx + 1) * P],
                                 kmb[:, 0:8], start=True, stop=False)
                        return e.matmul(
                            ps_p1b[:, 0:8],
                            qsb[par][:, BLK + sdx * P: BLK + (sdx + 1) * P],
                            kmb[:, 8:16], start=False, stop=True)
                    talt = None
                    if not (DBG & 64):
                        talt = em.op("tensor", altm,
                                     wait=[tkmb, prev_stt, p1b_free[0]],
                                     inc="s_mm")
                    tst = em.op("vector", lambda e, s, th=th: e.scalar_tensor_tensor(
                        out=sdiag[:], in0=ps_p1[:, 0:P], scalar=sc, in1=th[:],
                        op0=ALU.mult, op1=ALU.add), wait=[tdq, th_tok])
                    em.op("vector", lambda e, s: e.tensor_reduce(
                        m0c[:, 0:1], sdiag[:], AX.X, ALU.max))
                    if DBG & 64:
                        em.op("vector", lambda e, s: e.memset(altc[:, 0:1], -1e30))
                    else:
                        p1b_free[0] = em.op("vector", lambda e, s: e.tensor_scalar(
                            altc[:, 0:1], ps_p1b[:, 0:1],
                            sc, C, ALU.mult, ALU.add), wait=[talt], inc="s_v")
                    em.op("vector", lambda e, s: e.tensor_scalar_add(
                        t1c[:, 0:1], m0c[:, 0:1], 80.0))
                    em.op("vector", lambda e, s: e.tensor_tensor(
                        m0c[:, 0:1], m0c[:, 0:1], altc[:, 0:1], ALU.max))
                    if layer == 0 or B == 0:
                        em.op("vector", lambda e, s: e.tensor_tensor(
                            m0c[:, 0:1], m0c[:, 0:1], t1c[:, 0:1], ALU.min))
                    if layer == 0:
                        em.op("vector", lambda e, s, hsl=hsl: e.tensor_tensor(
                            m0c[:, 0:1], m0c[:, 0:1], r60[:, hsl:hsl + 1],
                            ALU.max))
                        em.op("vector", lambda e, s, q0=q0, sdx=sdx: e.tensor_scalar_add(
                            t1c[:, 0:1], iot_sb[:, 0:1], float(q0 + sdx * P)))
                        em.op("vector", lambda e, s, hsl=hsl: e.tensor_tensor(
                            t1c[:, 0:1], t1c[:, 0:1], slp_sb[:, hsl:hsl + 1],
                            ALU.mult))
                        em.op("vector", lambda e, s: e.tensor_tensor(
                            m0c[:, 0:1], m0c[:, 0:1], t1c[:, 0:1], ALU.add))
                        prev_stt = em.op("vector", lambda e, s, hh=hh, sdx=sdx: e.tensor_scalar_mul(
                            bcol[:, hh * 4 + sdx: hh * 4 + sdx + 1], m0c[:, 0:1],
                            -1.0 / sc), inc="s_v")
                    else:
                        prev_stt = em.op("vector", lambda e, s, hh=hh, sdx=sdx: e.tensor_scalar_mul(
                            bcol[:, hh * 4 + sdx: hh * 4 + sdx + 1], m0c[:, 0:1],
                            -1.0), inc="s_v")
            p1_free[0] = prev_stt

        def emit_pass1_B(idx):
            if DBG & 1:
                return
            g, B = tasks[idx]
            par = idx % 2
            if DBG & 32:
                for hh in range(nheads):
                    bias_tok[(idx, hh)] = em.op(
                        "vector", lambda e, s, hh=hh, par=par: e.memset(
                            bias_bc[hh][par][:], 0.0),
                        wait=[bbc_free.get((hh, idx)), p1_free[0]], inc="s_v")
                return
            bchain = p1_free[0]
            for hh in range(nheads):
                tpc = None
                for sdx in range(4):
                    tpc = em.op("tensor", lambda e, s, hh=hh, sdx=sdx: e.matmul(
                        ps_p1b[0:1, sdx * P:(sdx + 1) * P],
                        bcol[:, hh * 4 + sdx: hh * 4 + sdx + 1],
                        identb[:], start=True, stop=True),
                        wait=[bchain, cst, p1b_free[0]], inc="s_mm")
                trc = em.op("scalar", lambda e, s: e.activation(
                    biasrow[0:1, :], ps_p1b[0:1, 0:BLK], AF.Identity),
                    wait=[tpc], inc="s_a")
                tbc = em.op("tensor", lambda e, s: e.matmul(
                    ps_p1b[:, 0:BLK], onesr_b[0:1, :], biasrow[0:1, :],
                    start=True, stop=True), wait=[trc, gcst], inc="s_mm")
                p1b_free[0] = bias_tok[(idx, hh)] = em.op(
                    "vector", lambda e, s, hh=hh, par=par: e.tensor_copy(
                        bias_bc[hh][par][:], ps_p1b[:, 0:BLK]),
                    wait=[tbc, bbc_free.get((hh, idx))], inc="s_v")

        def emit_sweep(idx):
            g, B = tasks[idx]
            par = idx % 2
            injA = idx + 1 if idx + 1 < NT_ else None
            injB = injA
            last_smm = None
            pend_dpv = []

            def flush_dpv(keep=0):
                while len(pend_dpv) > keep:
                    fn, w, st = pend_dpv.pop(0)
                    pv_toks[st] = em.op("tensor", fn, wait=w, inc="s_mm")
            for si, sb in enumerate(range(B, -1, -1)):
                for ci in range(3, -1, -1):
                    in_blk = (sb == B)
                    ktc_abs = sb * 4 + ci
                    for hh in range(nheads):
                        hsl = g * 2 + hh
                        step = gstep[0]
                        jp = step % DEPTH
                        first = (si == 0 and ci == 3)
                        last = (sb == 0 and ci == 0)

                        def smm(e, s, jp=jp, hh=hh, par=par, ktc_abs=ktc_abs, g=g):
                            e.matmul(ps_s[jp][:], identb[:], bias_bc[hh][par][:],
                                     start=True, stop=False)
                            if layer == 0:
                                c = g * 2 + hh
                                return e.matmul(
                                    ps_s[jp][:],
                                    kres[:, c * S + ktc_abs * P: c * S + (ktc_abs + 1) * P],
                                    qsb[par][:, hh * BLK:(hh + 1) * BLK],
                                    start=False, stop=True)
                            e.matmul(ps_s[jp][:],
                                     kres[:, ktc_abs * P:(ktc_abs + 1) * P],
                                     qsb[par][:, 0:BLK], start=False, stop=False)
                            return e.matmul(
                                ps_s[jp][:],
                                kres[:, S + ktc_abs * P: S + (ktc_abs + 1) * P],
                                qsb[par][:, BLK:2 * BLK], start=False, stop=True)
                        waits = [bias_tok[(idx, hh)], tq_tok[idx], kv_tok,
                                 exp_toks.get(step - DEPTH)]
                        if first:
                            waits += ofin_tok.get(idx - 1, [])
                        tsm = em.op("tensor", smm, wait=waits, inc="s_mm")
                        last_smm = tsm

                        def expf(e, s, jp=jp, hsl=hsl, ktc_abs=ktc_abs):
                            if layer == 0:
                                col = hsl * NKC + ktc_abs
                                return e.activation(pt[jp][:], ps_s[jp][:],
                                                    AF.Exp,
                                                    bias=kvt_sb[:, col:col + 1],
                                                    scale=sc)
                            return e.activation(pt[jp][:], ps_s[jp][:], AF.Exp,
                                                bias=0.0, scale=1.0)
                        texp = em.op("scalar", expf, wait=[tsm], inc="s_a")
                        exp_toks[step] = texp
                        pready = texp
                        if in_blk:
                            if ci > 0:
                                em.op("vector", lambda e, s, jp=jp, ci=ci: e.memset(
                                    pt[jp][:, 0:ci * P], 0.0), wait=[texp])
                            pready = em.op("vector", lambda e, s, jp=jp, ci=ci: e.tensor_tensor(
                                pt[jp][:, ci * P:(ci + 1) * P],
                                pt[jp][:, ci * P:(ci + 1) * P], trim_sb[:],
                                ALU.mult), wait=[texp], inc="s_v")

                        def dpv(e, s, jp=jp, hh=hh, ktc_abs=ktc_abs, g=g,
                                first=first, last=last):
                            nd = 2 if layer == 0 else 1
                            if not (DBG & 4):
                                e.matmul(ps_d[0:nd, :], oneh[hh][:, 0:nd],
                                         pt[jp][:],
                                         start=(first and hh == 0),
                                         stop=(last and hh == nheads - 1))
                            if layer == 0:
                                return e.matmul(
                                    ps_o[hh][:],
                                    vres[:, ktc_abs * R + g * 256 + hh * P:
                                         ktc_abs * R + g * 256 + (hh + 1) * P],
                                    pt[jp][:], start=first, stop=last)
                            e.matmul(ps_o[0][:], vres[:, ktc_abs * R: ktc_abs * R + P],
                                     pt[jp][:], start=first, stop=last)
                            return e.matmul(
                                ps_o[1][:], vres[:, ktc_abs * R + P: ktc_abs * R + 2 * P],
                                pt[jp][:], start=first, stop=last)
                        flush_dpv(keep=1)
                        pend_dpv.append((dpv, [pready], step))
                        gstep[0] += 1
                if si == 0 and injA is not None:
                    emit_pass1_A(injA)
                elif si == 1 and injB is not None:
                    emit_pass1_B(injB)
                    injB = None
            flush_dpv()
            if injB is not None:
                emit_pass1_B(injB)
            qsb_free[idx + 2] = last_smm
            for hh in range(nheads):
                bbc_free[(hh, idx + 2)] = last_smm

            nd = 2 if layer == 0 else 1
            if DBG & 2:
                fins = []
                for hh in range(nheads):
                    for j in range(2 if layer == 1 else 1):
                        slot = hh if layer == 0 else j
                        po = ps_o[hh] if layer == 0 else ps_o[j]
                        tfin = em.op("vector", lambda e, s, slot=slot, po=po, par=par: e.tensor_copy(
                            ostg[par][:, slot * BLK:(slot + 1) * BLK], po[:]),
                            wait=[pv_toks[gstep[0] - 1],
                                  ost_tok.get((idx - 2, slot))], inc="s_fin")
                        fins.append(tfin)
                        ost_tok[(idx, slot)] = em.op(
                            "sync", lambda e, s, slot=slot, g=g, B=B, par=par: e.dma_start(
                                out=oh[g * 256 + slot * P: g * 256 + (slot + 1) * P,
                                       B * BLK:(B + 1) * BLK],
                                in_=ostg[par][:, slot * BLK:(slot + 1) * BLK]),
                            wait=[tfin], inc="s_st", incby=16)
                ofin_tok[idx] = fins
                return
            tdnf = em.op("scalar", lambda e, s, nd=nd: e.activation(
                denf[0:nd, :], ps_d[0:nd, :], AF.Identity),
                wait=[pv_toks[gstep[0] - 1]], inc="s_a")
            fins = []
            prev_rec = None
            for hh in range(nheads):
                tbc2 = em.op("tensor", lambda e, s, hh=hh, nd=nd: e.matmul(
                    ps_p1b[:, 0:BLK], hsel_sb[0:nd, hh * P:(hh + 1) * P],
                    denf[0:nd, :], start=True, stop=True),
                    wait=[tdnf, p1b_free[0], prev_rec], inc="s_mm")
                trec = em.op("vector", lambda e, s: e.reciprocal(
                    recf[:], ps_p1b[:, 0:BLK]), wait=[tbc2], inc="s_v")
                prev_rec = trec
                p1b_free[0] = trec
                for j in range(2 if layer == 1 else 1):
                    slot = hh if layer == 0 else j
                    po = ps_o[hh] if layer == 0 else ps_o[j]
                    tfin = em.op("vector", lambda e, s, slot=slot, po=po, par=par: e.tensor_tensor(
                        ostg[par][:, slot * BLK:(slot + 1) * BLK], po[:],
                        recf[:], ALU.mult),
                        wait=[ost_tok.get((idx - 2, slot))], inc="s_fin")
                    fins.append(tfin)
                    ost_tok[(idx, slot)] = em.op(
                        "sync", lambda e, s, slot=slot, g=g, B=B, par=par: e.dma_start(
                            out=oh[g * 256 + slot * P: g * 256 + (slot + 1) * P,
                                   B * BLK:(B + 1) * BLK],
                            in_=ostg[par][:, slot * BLK:(slot + 1) * BLK]),
                        wait=[tfin], inc="s_st", incby=16)
            ofin_tok[idx] = fins + [tdnf]

        emit_pass1_A(0)
        emit_pass1_B(0)
        for idx in range(NT_):
            emit_sweep(idx)
        em.op("sync", lambda e, s: e.nop(),
              wait=[("s_st", em.cnt.get("s_st", 0))])
        em.flush(nc, sems)
    return nc



def build_launch_PROJ(layer):
    Lp = L0 if layer == 0 else L1
    hd = Lp["hd"]
    nic = hd // P
    nc = bass.Bass("TRN2", target_bir_lowering=False)
    xsrc = nc.dram_tensor("xsrc", [TOK, D], F32, kind="ExternalInput").ap()
    ofm = nc.dram_tensor("ofm", [hd, TOK], BF16, kind="ExternalInput").ap()
    gfm = nc.dram_tensor("gfm", [D, TOK], BF16, kind="ExternalInput").ap()
    wpro = nc.dram_tensor("wpro", [hd, D], F32, kind="ExternalInput").ap()
    bprr = nc.dram_tensor("bprr", [P, IC], F32, kind="ExternalInput").ap()
    xout = nc.dram_tensor("xout", [TOK, D], F32, kind="ExternalOutput").ap()
    dfm_dram = nc.dram_tensor("dfm_dram", [D, TOK], BF16, kind="Internal").ap()

    es = ExitStack()
    with es:
        o_sb = es.enter_context(nc.sbuf_tensor("o_sb", [P, nic * TOK], BF16))
        bprr_sb = es.enter_context(nc.sbuf_tensor("bprr_sb", [P, IC], F32))
        wbuf = [es.enter_context(nc.sbuf_tensor(f"wbuf{j}", [P, nic * P], BF16)) for j in range(2)]
        gbuf = [es.enter_context(nc.sbuf_tensor(f"gbuf{j}", [P, TOK], BF16)) for j in range(2)]
        dstg = [es.enter_context(nc.sbuf_tensor(f"dstg{j}", [P, BLK], BF16)) for j in range(2)]
        ps = [es.enter_context(nc.psum_tensor(f"ps{j}", [P, BLK], F32)) for j in range(2)]
        sems = make_sems(nc, es, ["s_ld", "s_w", "s_mm", "s_v", "s_st", "s_kv"])
        em = Em()

        tb = em.op("sync", lambda e, s: e.dma_start(out=bprr_sb[:], in_=bprr),
                   inc="s_ld", incby=16)
        to = None
        for jj in range(4):
            to = em.op("sync", lambda e, s, jj=jj: e.dma_start(
                out=o_sb[:].rearrange("p (a t) -> p a t", t=TOK)[:, jj * (nic // 4):(jj + 1) * (nic // 4), :],
                in_=ofm.rearrange("(a p) t -> p a t", p=P)[:, jj * (nic // 4):(jj + 1) * (nic // 4), :]),
                inc="s_ld", incby=16)

        pm, pe, pst = {}, {}, {}
        for oc in range(IC):
            par = oc % 2
            lw = None
            for jj in range(4):
                lw = em.op("gpsimd", lambda e, s, oc=oc, par=par, jj=jj: e.dma_start(
                    out=wbuf[par][:].rearrange("p (a o) -> p a o", o=P)[:, jj * (nic // 4):(jj + 1) * (nic // 4), :],
                    in_=wpro.rearrange("(a p) o -> p a o", p=P)[:, jj * (nic // 4):(jj + 1) * (nic // 4), oc * P:(oc + 1) * P]),
                    wait=[pm.get((oc - 2, 1))] if jj == 0 else [], inc="s_w", incby=16)
            tg = em.op("gpsimd", lambda e, s, oc=oc, par=par: e.dma_start(
                out=gbuf[par][:], in_=gfm[oc * P:(oc + 1) * P, :]),
                wait=[pe.get((oc - 2, 1))], inc="s_w", incby=16)
            for bi in range(2):
                pp = bi

                def pmm(e, s, par=par, bi=bi, pp=pp):
                    r = None
                    for ic in range(nic):
                        r = e.matmul(ps[pp][:], wbuf[par][:, ic * P:(ic + 1) * P],
                                     o_sb[:, ic * TOK + bi * BLK: ic * TOK + bi * BLK + BLK],
                                     start=(ic == 0), stop=(ic == nic - 1))
                    return r
                tpm = em.op("tensor", pmm, wait=[lw, to, pe.get((oc - 2, bi))], inc="s_mm")
                pm[(oc, bi)] = tpm
                tpe = em.op("vector", lambda e, s, oc=oc, bi=bi, pp=pp, par=par: e.scalar_tensor_tensor(
                    out=dstg[pp][:], in0=ps[pp][:], scalar=bprr_sb[:, oc:oc + 1],
                    in1=gbuf[par][:, bi * BLK:(bi + 1) * BLK],
                    op0=ALU.add, op1=ALU.mult),
                    wait=[tpm, tg, tb, pst.get((oc - 2, bi))], inc="s_v")
                pe[(oc, bi)] = tpe
                pst[(oc, bi)] = em.op("sync", lambda e, s, oc=oc, bi=bi, pp=pp: e.dma_start(
                    out=dfm_dram[oc * P:(oc + 1) * P, bi * BLK:(bi + 1) * BLK],
                    in_=dstg[pp][:]), wait=[tpe], inc="s_st", incby=16)
        dall = em.tok("s_st")

        s3 = ExitStack()
        with s3:
            dtok = [s3.enter_context(nc.sbuf_tensor(f"dtok{j}", [P, D], BF16)) for j in range(2)]
            xtile = [s3.enter_context(nc.sbuf_tensor(f"xtile{j}", [P, D], F32)) for j in range(2)]
            otile = [s3.enter_context(nc.sbuf_tensor(f"otile{j}", [P, D], F32)) for j in range(2)]
            rtk = {}
            for tt in range(NT):
                par = tt % 2
                ttr = em.op("sync", lambda e, s, tt=tt, par=par: e.dma_start_transpose(
                    dtok[par][:], dfm_dram[:, tt * P:(tt + 1) * P]),
                    wait=[dall, rtk.get(tt - 2)], inc="s_kv", incby=16)
                tx = em.op("sync", lambda e, s, tt=tt, par=par: e.dma_start(
                    out=xtile[par][:], in_=xsrc[tt * P:(tt + 1) * P, :]),
                    inc="s_kv", incby=16)
                tadd = em.op("vector", lambda e, s, par=par: e.tensor_tensor(
                    otile[par][:], xtile[par][:], dtok[par][:], ALU.add),
                    wait=[ttr, tx], inc="s_v")
                rtk[tt] = em.op("sync", lambda e, s, tt=tt, par=par: e.dma_start(
                    out=xout[tt * P:(tt + 1) * P, :], in_=otile[par][:]),
                    wait=[tadd], inc="s_st", incby=16)
            em.op("sync", lambda e, s: e.nop(), wait=[("s_st", em.cnt.get("s_st", 0))])
            em.flush(nc, sems)
    return nc



_PROGS = {}


def _get_prog(key, builder):
    if key not in _PROGS:
        _PROGS[key] = builder()
    return _PROGS[key]


def _shard_rows(full):
    return [np.ascontiguousarray(np.concatenate(
        [full[c * BLK:(c + 1) * BLK], full[(15 - c) * BLK:(16 - c) * BLK]], axis=0))
        for c in range(8)]


def _unshard_rows(shards):
    full = np.empty((S,) + shards[0].shape[1:], dtype=shards[0].dtype)
    for c in range(8):
        full[c * BLK:(c + 1) * BLK] = shards[c][0:BLK]
        full[(15 - c) * BLK:(16 - c) * BLK] = shards[c][BLK:2 * BLK]
    return full


def _uncol_shards(mats):
    rows = mats[0].shape[0]
    full = np.empty((rows, S), dtype=mats[0].dtype)
    for c in range(8):
        full[:, c * BLK:(c + 1) * BLK] = mats[c][:, 0:BLK]
        full[:, (15 - c) * BLK:(16 - c) * BLK] = mats[c][:, BLK:2 * BLK]
    return full


def _col_shard(full):
    return [np.ascontiguousarray(np.concatenate(
        [full[:, c * BLK:(c + 1) * BLK], full[:, (15 - c) * BLK:(16 - c) * BLK]], axis=1))
        for c in range(8)]


def kernel(L0_attn_hidden_states, L0_attn_w_attn, L0_attn_b_attn, L0_attn_w_proj,
           L0_attn_b_proj, L0_ffn_weight, L1_attn_w_qkv, L1_attn_b_qkv,
           L1_attn_w_o, L1_attn_b_o, L1_ffn_A, L1_ffn_B, L1_ffn_Wr):
    f32 = np.float32
    x = np.ascontiguousarray(np.asarray(L0_attn_hidden_states, dtype=f32).reshape(S, D))
    x_sh = _shard_rows(x)

    w_attn = np.ascontiguousarray(np.asarray(L0_attn_w_attn, dtype=f32))
    barr0 = np.ascontiguousarray(np.asarray(L0_attn_b_attn, dtype=f32).reshape(96, P).T)
    w_proj = np.ascontiguousarray(np.asarray(L0_attn_w_proj, dtype=f32))
    bprr0 = np.ascontiguousarray(np.asarray(L0_attn_b_proj, dtype=f32).reshape(IC, P).T)
    wffn = np.ascontiguousarray(np.asarray(L0_ffn_weight, dtype=f32))
    wffnT = np.ascontiguousarray(wffn.T)
    wqkv1 = np.ascontiguousarray(np.asarray(L1_attn_w_qkv, dtype=f32).T)
    barr1 = np.ascontiguousarray(np.asarray(L1_attn_b_qkv, dtype=f32).reshape(48, P).T)
    wo1 = np.ascontiguousarray(np.asarray(L1_attn_w_o, dtype=f32).T)
    bprr1 = np.ascontiguousarray(np.asarray(L1_attn_b_o, dtype=f32).reshape(IC, P).T)
    A1 = np.asarray(L1_ffn_A, dtype=f32)
    B1 = np.asarray(L1_ffn_B, dtype=f32)
    Wr1 = np.asarray(L1_ffn_Wr, dtype=f32)
    aT = np.ascontiguousarray(A1.reshape(64, D).T)
    bP = np.ascontiguousarray(B1.transpose(0, 2, 1).reshape(64, D))
    wrT = np.ascontiguousarray(Wr1.T)
    oneh = np.zeros((4, 64), f32)
    for e in range(4):
        oneh[e, e * 16:(e + 1) * 16] = 1.0

    ii = np.arange(P, dtype=f32)
    iotac = np.ascontiguousarray(ii.reshape(P, 1))
    identb = np.ascontiguousarray(np.eye(P, dtype=ml_dtypes.bfloat16))
    hsel = np.zeros((2, 2 * P), f32)
    hsel[0, 0:P] = 1.0
    hsel[1, P:2 * P] = 1.0
    dmask = np.ascontiguousarray((ii[None, :] - ii[:, None]).astype(f32))
    mneg = np.ascontiguousarray(np.where(ii[None, :] > ii[:, None], NEG, 0.0).astype(f32))
    trim = np.ascontiguousarray(
        np.where(ii[:, None] <= ii[None, :], 1.0, 0.0).astype(ml_dtypes.bfloat16))

    cores = list(range(8))

    def run_A(layer, src_shards):
        ncp = _get_prog(("A", layer), lambda: build_launch_A(layer))
        maps = []
        for c in cores:
            m = {"src": src_shards[c]}
            if layer == 0:
                m.update(wqkv=w_attn, barr=barr0, wffn=wffn, wffnT=wffnT)
            else:
                m.update(wqkv=wqkv1, barr=barr1, wrT=wrT, aT=aT, bP=bP, oneh=oneh)
            maps.append(m)
        return run_bass_kernel_spmd(ncp, maps, core_ids=cores).results

    def run_ATT(layer, resA):
        Lp = L0 if layer == 0 else L1
        R = 256 * Lp["G"]
        Qabs = _uncol_shards([r["q_out"] for r in resA])
        Kabs = _uncol_shards([r["k_out"] for r in resA])
        Vsh = [r["v_out"] for r in resA]
        Vabs_sh = _unshard_rows(Vsh)
        ncp = _get_prog(("ATT", layer), lambda: build_launch_ATT(layer))
        maps = []
        kbase = np.arange(64, dtype=f32)[:, None] * P + np.arange(P, dtype=f32)[None, :]
        for c in cores:
            m = dict(
                qh=np.ascontiguousarray(Qabs[c * R:(c + 1) * R, :]),
                kh=np.ascontiguousarray(Kabs[c * R:(c + 1) * R, :]),
                vh=np.ascontiguousarray(Vabs_sh[:, c * R:(c + 1) * R]),
                mneg=mneg, trim=trim, identb=identb, hsel=hsel)
            if layer == 0:
                slp = np.ascontiguousarray(
                    np.tile(SLOPES[4 * c:4 * c + 4][None, :], (P, 1)).astype(f32))
                kvt = np.empty((P, 4 * 64), f32)
                for h in range(4):
                    kvt[:, h * 64:(h + 1) * 64] = (SLOPES[4 * c + h] * kbase).T
                m.update(slp=slp, dmask=dmask, iotac=iotac,
                         kvt=np.ascontiguousarray(kvt))
            maps.append(m)
        res = run_bass_kernel_spmd(ncp, maps, core_ids=cores).results
        Ofull = np.concatenate([r["oh"] for r in res], axis=0)
        return _col_shard(Ofull)

    def run_PROJ(layer, src_shards, o_shards, resA):
        ncp = _get_prog(("PROJ", layer), lambda: build_launch_PROJ(layer))
        maps = [dict(xsrc=src_shards[c], ofm=o_shards[c], gfm=resA[c]["g_out"],
                     wpro=(w_proj if layer == 0 else wo1),
                     bprr=(bprr0 if layer == 0 else bprr1)) for c in cores]
        res = run_bass_kernel_spmd(ncp, maps, core_ids=cores).results
        return [r["xout"] for r in res]

    resA0 = run_A(0, x_sh)
    o0_sh = run_ATT(0, resA0)
    x1_sh = run_PROJ(0, x_sh, o0_sh, resA0)
    resA1 = run_A(1, x1_sh)
    o1_sh = run_ATT(1, resA1)
    out_sh = run_PROJ(1, x1_sh, o1_sh, resA1)

    return _unshard_rows(out_sh).reshape(1, S, D).astype(np.float32)
